# version 20
# speedup vs baseline: 1.3279x; 1.0400x over previous
import numpy as np
import concourse.bass as bass
import concourse.mybir as mybir
from concourse import library_config as LCfg
from concourse.bass_utils import run_bass_kernel_spmd

F32 = mybir.dt.float32
BF16 = mybir.dt.bfloat16
U32 = mybir.dt.uint32
U16 = mybir.dt.uint16
I16 = mybir.dt.int16
AF = mybir.ActivationFunctionType
ALU = mybir.AluOpType
AX = mybir.AxisListType

T = 2048
D = 1024
NCH = D // 128
NTT = T // 128
NTB = T // 512
EPS_LN = 1e-5
EPS_RMS = 1e-5

BETA, THRESH, NSTEPS = 0.9, 1.0, 5

D_STATE = 128
D_CONV = 4
HEADDIM = 64
D_INNER = 2048
NHEADS = 32
CONV_DIM = D_INNER + 2 * D_STATE
D_IN_PROJ = 2 * D_INNER + 2 * D_STATE + NHEADS
NCONV_CH = CONV_DIM // 128
NIN_CH = (D_IN_PROJ + 127) // 128
NINNER_CH = D_INNER // 128


def _spike_breakpoints():
    def count(y):
        mem, c = 0.0, 0
        for _ in range(NSTEPS):
            reset = 1.0 if mem > THRESH else 0.0
            mem = BETA * mem + y - reset * THRESH
            c += 1 if mem > THRESH else 0
        return c
    bps = []
    for k in range(1, 6):
        lo, hi = -10.0, 10.0
        for _ in range(200):
            mid = 0.5 * (lo + hi)
            if count(mid) >= k:
                hi = mid
            else:
                lo = mid
        bps.append(0.5 * (lo + hi))
    return bps


SPIKE_BPS = _spike_breakpoints()


class Ctr:
    def __init__(self, handle, name):
        self.h = handle
        self.name = name
        self.n = 0

    def inc(self, k=1):
        self.n += k
        return self.n


class Prog:
    ENG = "tvsgy"

    def __init__(self, nc):
        self.nc = nc
        self.q = {e: [] for e in Prog.ENG}
        self.hw = {}

    def _emit(self, e, fn):
        self.q[e].append(fn)

    def t(self, fn): self._emit('t', fn)
    def v(self, fn): self._emit('v', fn)
    def s(self, fn): self._emit('s', fn)
    def g(self, fn): self._emit('g', fn)
    def y(self, fn): self._emit('y', fn)

    def wait(self, e, ctr, val):
        if val is None or val <= 0:
            return
        key = (e, ctr.name)
        if self.hw.get(key, -1) >= val:
            return
        self.hw[key] = val
        self._emit(e, lambda eng, h=ctr.h, v=val: eng.wait_ge(h, v))

    def run(self, block):
        @block.tensor
        def _(eng):
            for fn in self.q['t']:
                fn(eng)

        @block.vector
        def _(eng):
            for fn in self.q['v']:
                fn(eng)

        @block.scalar
        def _(eng):
            for fn in self.q['s']:
                fn(eng)

        @block.gpsimd
        def _(eng):
            for fn in self.q['g']:
                fn(eng)

        @block.sync
        def _(eng):
            for fn in self.q['y']:
                fn(eng)


class DmaQ:
    def __init__(self, P, nc, eng, name, n=8):
        self.P = P
        self.eng = eng
        self.slots = [Ctr(nc.alloc_semaphore(name=f"{name}{i}"), f"{name}{i}")
                      for i in range(n)]
        self.rr = 0

    def batch(self, fns):
        s = self.slots[self.rr]
        self.rr = (self.rr + 1) % len(self.slots)
        if s.n > 0:
            self.P.wait(self.eng, s, s.n)
        for fn in fns:
            s.inc(16)
            self.P._emit(self.eng, lambda e, f=fn, h=s.h: f(e).then_inc(h, 16))
        return (s, s.n)


class Bank:
    def __init__(self, tensor, idx):
        self.t = tensor
        self.idx = idx
        self.w = None
        self.reads = []


class Build:
    def __init__(self, nc):
        self.nc = nc
        self.P = Prog(nc)
        self.sem = {}
        for name in ("ts", "vs", "ss", "gs"):
            self.sem[name] = Ctr(nc.alloc_semaphore(name=name), name)
        self.esem = {'t': self.sem['ts'], 'v': self.sem['vs'],
                     's': self.sem['ss'], 'g': self.sem['gs']}
        self.Qy = DmaQ(self.P, nc, 'y', "qy")
        self.Qg = DmaQ(self.P, nc, 'g', "qg")
        self.banks = []
        self.sbuf_cursor = 16512
        self.named = {}

    def ti(self, fn):
        tok = self.sem['ts'].inc()
        self.P.t(lambda e, f=fn: f(e).then_inc(self.sem['ts'].h, 1))
        return tok

    def vi(self, fn):
        tok = self.sem['vs'].inc()
        self.P.v(lambda e, f=fn: f(e).then_inc(self.sem['vs'].h, 1))
        return tok

    def si(self, fn):
        tok = self.sem['ss'].inc()
        self.P.s(lambda e, f=fn: f(e).then_inc(self.sem['ss'].h, 1))
        return tok

    def gi(self, fn):
        tok = self.sem['gs'].inc()
        self.P.g(lambda e, f=fn: f(e).then_inc(self.sem['gs'].h, 1))
        return tok

    def alloc(self, name, shape, dtype, align=64, overlay=False):
        nbytes = int(np.prod(shape[1:])) * mybir.dt.size(dtype) if len(shape) > 1 else mybir.dt.size(dtype)
        if overlay:
            off = (self.ov_cursor + align - 1) // align * align
            assert off + nbytes <= self.ov_limit, f"overlay overflow at {name}"
            t = self.nc.alloc_sbuf_tensor_at(name, list(shape), dtype, offset=off)
            self.ov_cursor = off + nbytes
            self.named[name] = t
            return t
        off = (self.sbuf_cursor + align - 1) // align * align
        t = self.nc.alloc_sbuf_tensor_at(name, list(shape), dtype, offset=off)
        self.sbuf_cursor = off + nbytes
        assert self.sbuf_cursor <= 229260, f"SBUF overflow at {name}: {self.sbuf_cursor}"
        self.named[name] = t
        return t

    def overlay_region(self, start, limit):
        self.ov_cursor = start
        self.ov_limit = limit

    def mark(self):
        return self.sbuf_cursor

    def reset(self, m):
        self.sbuf_cursor = m
        self.guard = {k: self.sem[k].n for k in ("ts", "vs", "ss", "gs")}
        self.guard_dma = [(s, s.n) for s in self.Qy.slots + self.Qg.slots if s.n > 0]

    def soft_barrier(self, engines="tvsg"):
        cur = {k: self.sem[k].n for k in ("ts", "vs", "ss", "gs")}
        for e in engines:
            for k, v in cur.items():
                if self.esem.get(e) is not self.sem[k]:
                    self.P.wait(e, self.sem[k], v)

    def guard_wait(self, engines="tvsgy"):
        g = getattr(self, "guard", None)
        if not g:
            return
        for e in engines:
            for k, v in g.items():
                self.P.wait(e, self.sem[k], v)
            for (s, v) in getattr(self, "guard_dma", []):
                self.P.wait(e, s, v)
        self.P.v(lambda e: e.drain())
        self.P.s(lambda e: e.drain())
        self.P.g(lambda e: e.drain())

    def alloc_banks(self):
        for i in range(7):
            self.banks.append(Bank(self.nc.alloc_psum_tensor(f"pb{i}", [128, 512], F32), i))
        self.tbank = Bank(self.nc.alloc_psum_tensor("pb16", [128, 1024], BF16), 7)

    def bank_write(self, bank):
        for (e, ctr, tok) in bank.reads:
            self.P.wait('t', ctr, tok)
        if bank.w is not None and bank.w[0] != 't':
            self.P.wait('t', bank.w[1], bank.w[2])
        bank.reads = []

    def bank_written(self, bank, tok):
        bank.w = ('t', self.sem['ts'], tok)

    def bank_read(self, bank, e):
        if bank.w is not None and bank.w[0] != e:
            self.P.wait(e, bank.w[1], bank.w[2])
        for (re, ctr, tok) in bank.reads:
            if re != e:
                self.P.wait(e, ctr, tok)

    def bank_read_done(self, bank, e, tok):
        bank.reads.append((e, self.esem[e], tok))


def build_program(debug_taps=()):
    nc = bass.Bass(trn_type="TRN2", target_bir_lowering=False)
    B = Build(nc)
    P = B.P
    ts, vs, ss, gs = (B.sem[k] for k in ("ts", "vs", "ss", "gs"))
    Qy, Qg = B.Qy, B.Qg
    ti, vi, si, gi = B.ti, B.vi, B.si, B.gi

    def inp(name, shape, dtype=F32):
        return nc.dram_tensor(name, list(shape), dtype, kind="ExternalInput").ap()

    xT = inp("xT", [D, T])
    wqkv = inp("wqkv", [8, 128, 3 * NCH * 128])
    bqkv = inp("bqkv", [128, 4 * NCH])
    wo = inp("wo", [NCH, 128, 8 * 128])
    wfc = inp("wfc", [32, 128, NCH * 128])
    wproj = inp("wproj", [NCH, 128, 32 * 128])
    win = [inp(f"win{d}", [NCH, 128, NIN_CH * 128]) for d in range(2)]
    convdiag = [inp(f"convdiag{d}", [NCONV_CH, 128, D_CONV * 128]) for d in range(2)]
    convb = [inp(f"convb{d}", [128, NCONV_CH]) for d in range(2)]
    dtbias = [inp(f"dtbias{d}", [NHEADS, 1]) for d in range(2)]
    alog = [inp(f"alog{d}", [NHEADS, 1]) for d in range(2)]
    dskip = [inp(f"dskip{d}", [128, NINNER_CH]) for d in range(2)]
    mnormw = [inp(f"mnormw{d}", [128, NINNER_CH]) for d in range(2)]
    woutn = [inp(f"woutn{d}", [16, 128, NCH * 128]) for d in range(2)]
    projt = [inp(f"projt{d}", [NCH, 128, NCH * 128]) for d in range(2)]
    l1normw = inp("l1normw", [128, NCH])
    projb = inp("projb", [128, NCH])
    routerw = inp("routerw", [NCH, 128, 8])
    ebias_rep = inp("ebias_rep", [128, 8])
    iota8 = inp("iota8", [128, 8])
    sidx = nc.dram_tensor("sidx", [1, 8], mybir.dt.uint16, kind="ExternalInput").ap()
    w2n = inp("w2n", [8, 8, 128, 8 * 128])
    w13 = inp("w13", [9, 2, 8, 128, 8 * 128])
    w2 = inp("w2", [9, 8, 128, 8 * 128])
    tri_le16 = inp("tri_le16", [128, 128], BF16)
    tri_ge16 = inp("tri_ge16", [128, 128], BF16)
    maskf16 = inp("maskf16", [128, 128], BF16)
    maskb16 = inp("maskb16", [128, 128], BF16)
    tri_le32 = inp("tri_le32", [128, 128])
    tri_ge32 = inp("tri_ge32", [128, 128])
    ident16 = inp("ident16", [128, 128], BF16)
    identf = inp("identf", [128, 128])

    out = nc.dram_tensor("outT", [D, T], F32, kind="ExternalOutput").ap()
    xn1_dram = nc.dram_tensor("xn1_stash", [D, T], BF16, kind="Internal").ap()

    taps = {}

    def tap(name, shape, dtype=F32):
        if name in debug_taps:
            taps[name] = nc.dram_tensor("tap_" + name, list(shape), dtype,
                                        kind="ExternalOutput").ap()
            return taps[name]
        return None

    B.alloc_banks()
    pb = B.banks
    tbk = B.tbank

    x0_base = B.sbuf_cursor
    x0 = [B.alloc(f"x0_{c}", [128, T], F32) for c in range(NCH)]
    x0_end = B.sbuf_cursor
    xnb = [B.alloc(f"xnb_{c}", [128, T], BF16) for c in range(NCH)]
    ones16 = B.alloc("ones16", [128, 128], BF16)
    onesf = B.alloc("onesf", [128, 128], F32)
    id16 = B.alloc("id16", [128, 128], BF16)
    idf = B.alloc("idf", [128, 128], F32)
    trile16 = B.alloc("trile16", [128, 128], BF16)
    trige16 = B.alloc("trige16", [128, 128], BF16)
    cpool = B.alloc("cpool", [128, 16], F32)
    bias_sb = B.alloc("bias_sb", [128, 4 * NCH], F32)
    stat = B.alloc("ln_stat", [128, 64], F32)
    maskf = B.alloc("maskf", [128, 128], BF16)
    maskb = B.alloc("maskb", [128, 128], BF16)
    trile32 = B.alloc("trile32", [128, 128], F32)
    trige32 = B.alloc("trige32", [128, 128], F32)
    l1par = B.alloc("l1par", [128, 66], F32)
    rms_rstd = B.alloc("rms_rstd", [128, 16], F32)

    CONSTS = [EPS_LN, -0.5, -12.0, 0.125, 1.0, 0.0, 0.2, 1.0 / D]
    for i, cval in enumerate(CONSTS):
        nc.const_aps.register(F32, float(cval), cpool[:, i:i + 1])

    def init_consts():
        for i, cval in enumerate(CONSTS):
            P.v(lambda e, i=i, cv=float(cval): e.memset(cpool[:, i:i + 1], cv))
        P.v(lambda e: e.memset(ones16[:], 1.0))
        vi(lambda e: e.memset(onesf[:], 1.0))
        fns = []
        for srct, dst in ((tri_le16, trile16), (tri_ge16, trige16),
                          (ident16, id16), (identf, idf),
                          (maskf16, maskf), (maskb16, maskb),
                          (tri_le32, trile32), (tri_ge32, trige32)):
            fns.append(lambda e, s=srct, d=dst: e.dma_start(d[:], s[:, :]))
        fns.append(lambda e: e.dma_start(bias_sb[:], bqkv[:, :]))
        dtok = Qy.batch(fns)
        ctok = vs.n
        for eng in "tvsg":
            P.wait(eng, vs, ctok)
            P.wait(eng, dtok[0], dtok[1])
        return dtok

    def load_x():
        return Qy.batch([
            (lambda e, c=c: e.dma_start(x0[c][:], xT[c * 128:(c + 1) * 128, :]))
            for c in range(NCH)])

    def wload(dst_ap, src_ap, free=None):
        if free is not None:
            P.wait('g', free[0], free[1])
        return Qg.batch([lambda e, d=dst_ap, s=src_ap: e.dma_start(d, s)])

    def layer_norm(x_wait, stash=False):
        m0 = B.mark()
        sq = [B.alloc(f"ln_sq_{c}", [128, T], BF16) for c in range(NCH)]
        B.guard_wait()
        bankA, bankB, bankR, bankM = pb[0], pb[1], pb[2], pb[3]
        ctok = stok = 0
        for c in range(NCH):
            if x_wait is not None:
                P.wait('v', x_wait[0], x_wait[1])
                P.wait('s', x_wait[0], x_wait[1])
            ctok = vi(lambda e, c=c: e.tensor_copy(xnb[c][:], x0[c][:]))
            stok = si(lambda e, c=c: e.activation(sq[c][:], x0[c][:], AF.Square))
        B.bank_write(bankA)
        B.bank_write(bankB)
        P.wait('t', vs, ctok)
        P.wait('t', ss, stok)
        mtok = None
        for tt in range(NTT):
            for c in range(NCH):
                sl = slice(tt * 128, (tt + 1) * 128)
                P.t(lambda e, c=c, tt=tt, sl=sl: e.matmul(
                    bankA.t[:, tt:tt + 1], xnb[c][:, sl], ones16[:, 0:1],
                    start=(c == 0), stop=(c == NCH - 1)))
            for c in range(NCH):
                sl = slice(tt * 128, (tt + 1) * 128)
                if tt == NTT - 1 and c == NCH - 1:
                    mtok = ti(lambda e, c=c, tt=tt, sl=sl: e.matmul(
                        bankB.t[:, tt:tt + 1], sq[c][:, sl], ones16[:, 0:1],
                        start=(c == 0), stop=True))
                else:
                    P.t(lambda e, c=c, tt=tt, sl=sl: e.matmul(
                        bankB.t[:, tt:tt + 1], sq[c][:, sl], ones16[:, 0:1],
                        start=(c == 0), stop=(c == NCH - 1)))
        B.bank_written(bankA, mtok)
        B.bank_written(bankB, mtok)
        B.bank_read(bankA, 'v')
        B.bank_read(bankB, 'v')
        P.v(lambda e: e.tensor_scalar(stat[:, 0:16], bankA.t[:, 0:16], 1.0 / D, None, ALU.mult))
        rtok = vi(lambda e: e.tensor_scalar(stat[:, 16:32], bankB.t[:, 0:16], 1.0 / D, None, ALU.mult))
        B.bank_read_done(bankA, 'v', rtok)
        B.bank_read_done(bankB, 'v', rtok)
        P.v(lambda e: e.drain())
        P.v(lambda e: e.scalar_tensor_tensor(
            stat[:, 48:64], stat[:, 0:16], -1.0, stat[:, 0:16], ALU.mult, ALU.mult))
        P.v(lambda e: e.drain())
        vtok = vi(lambda e: e.tensor_add(stat[:, 16:32], stat[:, 16:32], stat[:, 48:64]))
        P.wait('s', vs, vtok)
        P.s(lambda e: e.activation(stat[:, 16:32], stat[:, 16:32], AF.Ln, bias=float(EPS_LN)))
        P.s(lambda e: e.drain())
        sstok = si(lambda e: e.activation(stat[:, 32:48], stat[:, 16:32], AF.Exp, scale=-0.5))
        P.wait('v', ss, sstok)
        mrstok = vi(lambda e: e.tensor_mul(stat[:, 48:64], stat[:, 0:16], stat[:, 32:48]))
        for tb in range(NTB):
            B.bank_write(bankR)
            B.bank_write(bankM)
            P.wait('t', vs, mrstok)
            for j in range(4):
                tt = tb * 4 + j
                rep_r = bass.AP(stat, 32 + tt, [[64, 128], [0, 128]])
                P.t(lambda e, r=rep_r, j=j: e.matmul(
                    bankR.t[:, j * 128:(j + 1) * 128], r, idf[:],
                    start=True, stop=True, is_transpose=True))
            ttok2 = None
            for j in range(4):
                tt = tb * 4 + j
                rep_m = bass.AP(stat, 48 + tt, [[64, 128], [0, 128]])
                ttok2 = ti(lambda e, m=rep_m, j=j: e.matmul(
                    bankM.t[:, j * 128:(j + 1) * 128], m, idf[:],
                    start=True, stop=True, is_transpose=True))
            B.bank_written(bankR, ttok2)
            B.bank_written(bankM, ttok2)
            B.bank_read(bankR, 'v')
            B.bank_read(bankM, 'v')
            sl = slice(tb * 512, (tb + 1) * 512)
            vdone = None
            for c in range(NCH):
                P.v(lambda e, c=c, sl=sl: e.tensor_mul(xnb[c][:, sl], x0[c][:, sl], bankR.t[:]))
                P.v(lambda e: e.drain())
                vdone = vi(lambda e, c=c, sl=sl: e.tensor_sub(xnb[c][:, sl], xnb[c][:, sl], bankM.t[:]))
                P.v(lambda e: e.drain())
            B.bank_read_done(bankR, 'v', vdone)
            B.bank_read_done(bankM, 'v', vdone)
        xn_tok = vs.n
        B.reset(m0)
        stash_tok = None
        if stash:
            P.wait('y', vs, xn_tok)
            stash_tok = Qy.batch([
                (lambda e, c=c: e.dma_start(xn1_dram[c * 128:(c + 1) * 128, :], xnb[c][:]))
                for c in range(NCH)])
        return xn_tok, stash_tok

    def l0_attention(xn_tok):
        m0 = B.mark()
        q16 = [B.alloc(f"q16_{i}", [128, T], BF16) for i in range(2)]
        k16 = [B.alloc(f"k16_{i}", [128, T], BF16) for i in range(2)]
        v16 = [B.alloc(f"v16_{i}", [128, T], BF16) for i in range(2)]
        wqb = [B.alloc(f"wqb_{i}", [128, 3 * NCH * 128], BF16) for i in range(3)]
        vnat = B.alloc("vnat", [128, 16 * 130], BF16)
        ctx_all = [B.alloc(f"ctx_{p}", [128, T], BF16) for p in range(8)]
        ctxodd = B.alloc("ctxodd", [64, T], BF16)
        Ebuf = [B.alloc(f"Ebuf_{i}", [128, 512], BF16) for i in range(4)]
        denrow = B.alloc("denrow", [128, 1024], BF16)
        recb = B.alloc("recb", [64, 1024], F32)
        wob = [B.alloc(f"wob_{i}", [128, 8 * 128], BF16) for i in range(2)]
        B.guard_wait()

        SE = [pb[0], pb[1]]
        SO = [pb[2], pb[3]]
        CE, CO, BN = pb[4], pb[5], pb[6]

        P.wait('v', vs, xn_tok)
        vnat_ones = vi(lambda e: e.memset(vnat[:], 1.0))

        wq_tok = [None, None, None]
        wq_free = [None, None, None]
        wq_tok[0] = wload(wqb[0][:], wqkv[0])
        wq_tok[1] = wload(wqb[1][:], wqkv[1])
        ctx_done = [None] * 8
        mv_prev = None

        def qkv_groups(p, cur, cw):
            dsts = (q16[cur], k16[cur], v16[cur])
            out = []
            for w in range(3):
                for tb in range(NTB):
                    def emit(w=w, tb=tb, cur=cur, cw=cw, p=p, dsts=dsts):
                        bank = BN
                        B.bank_write(bank)
                        P.wait('t', wq_tok[cw][0], wq_tok[cw][1])
                        P.wait('t', vs, xn_tok)
                        mtok = None
                        for c in range(NCH):
                            fn = (lambda e, w=w, c=c, tb=tb, bank=bank, cw=cw: e.matmul(
                                bank.t[:], wqb[cw][:, (w * NCH + c) * 128:(w * NCH + c + 1) * 128],
                                xnb[c][:, tb * 512:(tb + 1) * 512],
                                start=(c == 0), stop=(c == NCH - 1)))
                            if c == NCH - 1:
                                mtok = ti(fn)
                            else:
                                P.t(fn)
                        B.bank_written(bank, mtok)
                        B.bank_read(bank, 'v')
                        qtok = vi(lambda e, dst=dsts[w], w=w, tb=tb, bank=bank, p=p: e.tensor_scalar(
                            dst[:, tb * 512:(tb + 1) * 512], bank.t[:],
                            bias_sb[:, 4 * p + w:4 * p + w + 1], None, ALU.add))
                        B.bank_read_done(bank, 'v', qtok)
                        return qtok
                    out.append(emit)
            return out

        pend = qkv_groups(0, 0, 0)
        qkv_done = [None] * 8
        for emit in pend:
            qkv_done[0] = emit()
        pend = []

        for p in range(8):
            cur = p % 2
            if p + 2 < 8:
                s3 = (p + 2) % 3
                wq_tok[s3] = wload(wqb[s3][:], wqkv[p + 2], free=wq_free[s3])
            if p + 1 < 8:
                pend = qkv_groups(p + 1, 1 - cur, (p + 1) % 3)
            P.wait('t', vs, qkv_done[p])
            if p == 0:
                P.wait('v', vs, vnat_ones)
            vnat_tok = None
            for kt in range(NTT):
                sl0 = (kt % 2) * 512
                B.bank_write(tbk)
                ttok = ti(lambda e, kt=kt, cur=cur, sl0=sl0: e.transpose(
                    tbk.t[:, sl0:sl0 + 128], v16[cur][:, kt * 128:(kt + 1) * 128], id16[:]))
                B.bank_written(tbk, ttok)
                B.bank_read(tbk, 'v')
                P.v(lambda e, kt=kt, sl0=sl0: e.tensor_copy(
                    vnat[:, kt * 130:kt * 130 + 64], tbk.t[:, sl0:sl0 + 64]))
                vnat_tok = vi(lambda e, kt=kt, sl0=sl0: e.tensor_copy(
                    vnat[:, kt * 130 + 65:kt * 130 + 129], tbk.t[:, sl0 + 64:sl0 + 128]))
                B.bank_read_done(tbk, 'v', vnat_tok)
            if mv_prev is not None:
                P.wait('v', mv_prev[0], mv_prev[1])
            for qb in range(NTB):
                qsl = slice(qb * 512, (qb + 1) * 512)
                for _ in range(3):
                    if pend:
                        qkv_done[p + 1] = pend.pop(0)()
                B.bank_write(CE)
                B.bank_write(CO)
                e_free = [0, 0, 0, 0]
                ctx_mtok = None
                exp_toks = {}
                def emit_scores(kt):
                    se, so = SE[kt % 2], SO[kt % 2]
                    ksl = slice(kt * 128, (kt + 1) * 128)
                    B.bank_write(se)
                    P.wait('t', vs, vnat_tok)
                    tok_e = ti(lambda e, cur=cur, ksl=ksl, qsl=qsl, se=se: e.matmul(
                        se.t[:], k16[cur][0:64, ksl], q16[cur][0:64, qsl],
                        start=True, stop=True))
                    B.bank_written(se, tok_e)
                    B.bank_write(so)
                    tok_o = ti(lambda e, cur=cur, ksl=ksl, qsl=qsl, so=so: e.matmul(
                        so.t[:], k16[cur][64:128, ksl], q16[cur][64:128, qsl],
                        start=True, stop=True))
                    B.bank_written(so, tok_o)
                    for par, bank in ((0, se), (1, so)):
                        eb = Ebuf[par * 2 + kt % 2]
                        B.bank_read(bank, 's')
                        P.wait('s', ts, e_free[par * 2 + kt % 2])
                        stok = si(lambda e, eb=eb, bank=bank: e.activation(
                            eb[:], bank.t[:], AF.Exp, bias=-12.0, scale=0.125))
                        B.bank_read_done(bank, 's', stok)
                        exp_toks[(kt, par)] = stok
                def emit_ctx(kt):
                    nonlocal ctx_mtok
                    for par in (0, 1):
                        eb = Ebuf[par * 2 + kt % 2]
                        stok = exp_toks.pop((kt, par))
                        cb = CE if par == 0 else CO
                        v_lo = kt * 130 + (0 if par == 0 else 65)
                        P.wait('t', ss, stok)
                        tok_c = ti(lambda e, eb=eb, cb=cb, v_lo=v_lo, kt=kt: e.matmul(
                            cb.t[0:65, :], vnat[:, v_lo:v_lo + 65], eb[:],
                            start=(kt == 0), stop=(kt == NTT - 1)))
                        e_free[par * 2 + kt % 2] = tok_c
                        ctx_mtok = tok_c
                emit_scores(0)
                for kt in range(1, NTT):
                    emit_scores(kt)
                    emit_ctx(kt - 1)
                emit_ctx(NTT - 1)
                B.bank_written(CE, ctx_mtok)
                B.bank_written(CO, ctx_mtok)
                B.bank_read(CE, 'v')
                B.bank_read(CO, 'v')
                P.v(lambda e: e.tensor_copy(denrow[64:65, 0:512], CE.t[64:65, :]))
                dtok = vi(lambda e: e.tensor_copy(denrow[64:65, 512:1024], CO.t[64:65, :]))
                DBE, DBO = SE[0], SO[0]
                B.bank_write(DBE)
                B.bank_write(DBO)
                P.wait('t', vs, dtok)
                be = ti(lambda e, DBE=DBE: e.matmul(DBE.t[0:64, :], ones16[64:65, 0:64],
                                                    denrow[64:65, 0:512], start=True, stop=True))
                B.bank_written(DBE, be)
                bo_ = ti(lambda e, DBO=DBO: e.matmul(DBO.t[64:128, :], ones16[64:65, 64:128],
                                                     denrow[64:65, 512:1024], start=True, stop=True))
                B.bank_written(DBO, bo_)
                B.bank_read(DBE, 'v')
                B.bank_read(DBO, 'v')
                P.v(lambda e, DBE=DBE: e.reciprocal(recb[:, 0:512], DBE.t[0:64, :]))
                rtok = vi(lambda e, DBO=DBO: e.reciprocal(recb[:, 512:1024], DBO.t[64:128, :]))
                B.bank_read_done(DBE, 'v', rtok)
                B.bank_read_done(DBO, 'v', rtok)
                P.v(lambda e: e.drain())
                P.v(lambda e, p=p, qsl=qsl: e.tensor_mul(
                    ctx_all[p][0:64, qsl], CE.t[0:64, :], recb[:, 0:512]))
                ntok = vi(lambda e, qsl=qsl: e.tensor_mul(
                    ctxodd[:, qsl], CO.t[0:64, :], recb[:, 512:1024]))
                B.bank_read_done(CE, 'v', ntok)
                B.bank_read_done(CO, 'v', ntok)
            while pend:
                qkv_done[p + 1] = pend.pop(0)()
            P.wait('y', vs, vs.n)
            mv_prev = Qy.batch([lambda e, p=p: e.dma_start(ctx_all[p][64:128, :], ctxodd[:, :])])
            ctx_done[p] = mv_prev
            wq_free[p % 3] = (ts, ts.n)

        wo_tok = [None, None]
        wo_free = [None, None]
        wo_tok[0] = wload(wob[0][:], wo[0])
        for c in range(NCH):
            cur = c % 2
            if c + 1 < NCH:
                wo_tok[1 - cur] = wload(wob[1 - cur][:], wo[c + 1], free=wo_free[1 - cur])
            P.wait('t', wo_tok[cur][0], wo_tok[cur][1])
            for p in range(8):
                P.wait('t', ctx_done[p][0], ctx_done[p][1])
            for tb in range(NTB):
                bank = SE[tb % 2]
                B.bank_write(bank)
                mtok = None
                for p in range(8):
                    fn = (lambda e, c=c, p=p, tb=tb, bank=bank, cur=cur: e.matmul(
                        bank.t[:], wob[cur][:, p * 128:(p + 1) * 128],
                        ctx_all[p][:, tb * 512:(tb + 1) * 512],
                        start=(p == 0), stop=(p == 7)))
                    if p == 7:
                        mtok = ti(fn)
                    else:
                        P.t(fn)
                B.bank_written(bank, mtok)
                B.bank_read(bank, 'v')
                vtok = vi(lambda e, c=c, tb=tb, bank=bank: e.scalar_tensor_tensor(
                    x0[c][:, tb * 512:(tb + 1) * 512], bank.t[:],
                    bias_sb[:, 4 * c + 3:4 * c + 4], x0[c][:, tb * 512:(tb + 1) * 512],
                    ALU.add, ALU.add))
                B.bank_read_done(bank, 'v', vtok)
            wo_free[cur] = (ts, ts.n)
        attn_tok = vs.n
        B.reset(m0)
        return attn_tok

    def l0_ffn(xn_tok):
        m0 = B.mark()
        h_sb = [B.alloc(f"ffn_h_{fc}", [128, 1024], BF16) for fc in range(32)]
        wfcb = [B.alloc(f"wfcb_{i}", [128, NCH * 128], BF16) for i in range(3)]
        wpjb = [B.alloc(f"wpjb_{i}", [128, 32 * 128], BF16) for i in range(2)]
        relu16 = B.alloc("relu16", [128, 1024], BF16)
        y16 = B.alloc("ffn_y16", [128, 512], BF16)
        acc = B.alloc("ffn_acc", [128, 512], F32)
        B.guard_wait()
        PA = [pb[0], pb[1]]
        PBk = [pb[2], pb[3]]
        P.wait('t', vs, xn_tok)
        bps = SPIKE_BPS
        sq_prev = [0, 0]
        for half in range(2):
            hof = half * 1024
            if half == 1:
                P.wait('s', ts, ts.n)
                P.wait('v', ts, ts.n)
            wfc_tok = [None] * 3
            wfc_free = [None] * 3
            wfc_tok[0] = wload(wfcb[0][:], wfc[half * 0 + 0], free=None if half == 0 else wfc_free_prev[0])
            for fc in range(32):
                r = fc % 3
                if fc + 1 < 32:
                    rn = (fc + 1) % 3
                    wfc_tok[rn] = wload(wfcb[rn][:], wfc[fc + 1], free=wfc_free[rn])
                P.wait('t', wfc_tok[r][0], wfc_tok[r][1])
                sq_tok = 0
                for th in range(2):
                    bank = PA[th]
                    B.bank_write(bank)
                    mtok = None
                    for c in range(NCH):
                        fn = (lambda e, fc=fc, c=c, th=th, bank=bank, r=r, hof=hof: e.matmul(
                            bank.t[:], wfcb[r][:, c * 128:(c + 1) * 128],
                            xnb[c][:, hof + th * 512:hof + (th + 1) * 512],
                            start=(c == 0), stop=(c == NCH - 1)))
                        if c == NCH - 1:
                            mtok = ti(fn)
                        else:
                            P.t(fn)
                    B.bank_written(bank, mtok)
                    B.bank_read(bank, 'v')
                    P.wait('v', ss, sq_prev[th])
                    rtok = vi(lambda e, th=th, bank=bank: e.tensor_scalar(
                        relu16[:, th * 512:(th + 1) * 512], bank.t[:], 0.0, None, ALU.max))
                    B.bank_read_done(bank, 'v', rtok)
                    P.wait('s', vs, rtok)
                    sq_tok = si(lambda e, fc=fc, th=th: e.activation(
                        h_sb[fc][:, th * 512:(th + 1) * 512], relu16[:, th * 512:(th + 1) * 512],
                        AF.Square))
                    sq_prev[th] = sq_tok
                wfc_free[r] = (ss, sq_tok)
            wfc_free_prev = wfc_free
            h_tok = sq_tok
            wpj_tok = [None, None]
            wpj_free = [None, None]
            wpj_tok[0] = wload(wpjb[0][:], wproj[0], free=None if half == 0 else wpj_free_prev[0])
            for dc in range(NCH):
                cur = dc % 2
                if dc + 1 < NCH:
                    wpj_tok[1 - cur] = wload(wpjb[1 - cur][:], wproj[dc + 1],
                                             free=wpj_free[1 - cur] if not (half == 1 and dc == 0) else wpj_free_prev[1])
                P.wait('t', wpj_tok[cur][0], wpj_tok[cur][1])
                P.wait('t', ss, h_tok)
                for th in range(2):
                    bank = PBk[th]
                    B.bank_write(bank)
                    mtok = None
                    for fc in range(32):
                        fn = (lambda e, dc=dc, fc=fc, th=th, bank=bank, cur=cur: e.matmul(
                            bank.t[:], wpjb[cur][:, fc * 128:(fc + 1) * 128],
                            h_sb[fc][:, th * 512:(th + 1) * 512],
                            start=(fc == 0), stop=(fc == 31)))
                        if fc == 31:
                            mtok = ti(fn)
                        else:
                            P.t(fn)
                    B.bank_written(bank, mtok)
                    B.bank_read(bank, 'v')
                    P.v(lambda e, bank=bank: e.tensor_copy(y16[:], bank.t[:]))
                    P.v(lambda e: e.drain())
                    ytok = vi(lambda e, bank=bank, b0=bps[0]: e.tensor_scalar(
                        acc[:], y16[:], b0, None, ALU.is_gt))
                    B.bank_read_done(bank, 'v', ytok)
                    for bi in bps[1:]:
                        P.v(lambda e: e.drain())
                        P.v(lambda e, bi=bi: e.scalar_tensor_tensor(
                            acc[:], y16[:], bi, acc[:], ALU.is_gt, ALU.add))
                    P.v(lambda e: e.drain())
                    vtok = vi(lambda e, dc=dc, th=th, hof=hof: e.scalar_tensor_tensor(
                        x0[dc][:, hof + th * 512:hof + (th + 1) * 512], acc[:], 0.2,
                        x0[dc][:, hof + th * 512:hof + (th + 1) * 512], ALU.mult, ALU.add))
                    P.v(lambda e: e.drain())
                wpj_free[cur] = (ts, ts.n)
            wpj_free_prev = wpj_free
        ffn_tok = vs.n
        B.reset(m0)
        return ffn_tok

    ydram = [nc.dram_tensor(f"ydram{d}", [D_INNER, T], BF16, kind="Internal").ap()
             for d in range(2)]
    xnt_h = nc.dram_tensor("xnt_dram", [T + 1, D], BF16, kind="Internal")
    xnt_dram = xnt_h.ap()
    ymoe_h = nc.dram_tensor("ymoe_dram", [T + 1, D], BF16, kind="Internal")
    ymoe_dram = ymoe_h.ap()

    def l1_prep(xn_tok):
        m0 = B.mark()
        sq = [B.alloc(f"rms_sq_{c}", [128, T], BF16) for c in range(NCH)]
        B.guard_wait()
        l1n_tok = Qy.batch([lambda e: e.dma_start(l1par[:, 50:58], l1normw[:, :]),
                            lambda e: e.dma_start(l1par[:, 58:66], projb[:, :])])
        P.wait('v', l1n_tok[0], l1n_tok[1])
        P.wait('v', vs, xn_tok)
        P.wait('s', vs, xn_tok)
        stok = 0
        for c in range(NCH):
            for tb in range(NTB):
                P.v(lambda e, c=c, tb=tb: e.scalar_tensor_tensor(
                    x0[c][:, tb * 512:(tb + 1) * 512], xnb[c][:, tb * 512:(tb + 1) * 512],
                    l1par[:, 58 + c:59 + c], x0[c][:, tb * 512:(tb + 1) * 512],
                    ALU.add, ALU.add))
            P.v(lambda e: e.drain())
            stok = si(lambda e, c=c: e.activation(sq[c][:], xnb[c][:], AF.Square))
        resid_tok = vi(lambda e: e.drain())
        bankA = pb[0]
        B.bank_write(bankA)
        P.wait('t', ss, stok)
        mtok = None
        for tt in range(NTT):
            for c in range(NCH):
                sl = slice(tt * 128, (tt + 1) * 128)
                if tt == NTT - 1 and c == NCH - 1:
                    mtok = ti(lambda e, c=c, tt=tt, sl=sl: e.matmul(
                        bankA.t[:, tt:tt + 1], sq[c][:, sl], ones16[:, 0:1],
                        start=(c == 0), stop=True))
                else:
                    P.t(lambda e, c=c, tt=tt, sl=sl: e.matmul(
                        bankA.t[:, tt:tt + 1], sq[c][:, sl], ones16[:, 0:1],
                        start=(c == 0), stop=(c == NCH - 1)))
        B.bank_written(bankA, mtok)
        B.bank_read(bankA, 'v')
        vtok = vi(lambda e: e.tensor_scalar(rms_rstd[:], bankA.t[:, 0:16], 1.0 / D, None, ALU.mult))
        B.bank_read_done(bankA, 'v', vtok)
        P.wait('s', vs, vtok)
        P.s(lambda e: e.activation(rms_rstd[:], rms_rstd[:], AF.Ln, bias=float(EPS_RMS)))
        P.s(lambda e: e.drain())
        rtok = si(lambda e: e.activation(rms_rstd[:], rms_rstd[:], AF.Exp, scale=-0.5))
        B.reset(m0)
        return rtok, resid_tok

    def l1_mamba(d, rms_tok, stash_tok=None):
        m0 = B.mark()
        B.overlay_region(x0_base, x0_end)
        winb = [B.alloc(f"winb_{c}", [128, D_IN_PROJ], BF16, overlay=(c < 7))
                for c in range(NCH)]
        cdiag = [B.alloc(f"cdiag_{i}", [128, D_CONV * 128], BF16) for i in range(NCONV_CH)]
        rawb = [B.alloc(f"raw_{i}", [128, 259], BF16) for i in range(NCONV_CH)]
        zsilp = [[B.alloc(f"zsil_{j}_{i}", [128, 256], BF16) for i in range(NINNER_CH)]
                 for j in range(2)]
        cvo = [B.alloc(f"cvo_{i}", [128, 256], BF16) for i in range(NCONV_CH)]
        cvo17p = [cvo[17], B.alloc("cvo17x", [128, 256], BF16)]
        rhs16 = [B.alloc(f"mrhs_{c}", [128, 256], BF16) for c in range(NCH)]
        dtbuf = B.alloc("dtbuf", [32, 3 * 256], F32)
        dtb_sb = B.alloc("dtb_sb", [32, 1], F32)
        alog_sb = B.alloc("alog_sb", [32, 2], F32)
        sct2 = [B.alloc(f"sct_{i}", [128, 128], F32) for i in range(2)]
        cpcol2 = [B.alloc(f"cpcol_{i}", [128, 32], F32) for i in range(2)]
        Lp4 = [B.alloc(f"Lp4_{i}", [128, 512], BF16) for i in range(2)]
        GL4 = [B.alloc(f"GL4_{i}", [128, 512], BF16) for i in range(2)]
        Ccp4 = [B.alloc(f"Ccp4_{i}", [128, 512], BF16) for i in range(2)]
        cpe4 = [B.alloc(f"cpe4_{i}", [128, 4], F32) for i in range(2)]
        wcol4 = [B.alloc(f"wcol4_{i}", [128, 4], F32) for i in range(2)]
        G16_2 = [B.alloc(f"G16_{i}", [128, 128], BF16) for i in range(2)]
        Bnat2 = [B.alloc(f"Bnat_{i}", [128, 128], BF16) for i in range(2)]
        xdnat2 = [[B.alloc(f"xdnat_{i}_{j}", [128, 128], BF16) for i in range(NINNER_CH)]
                  for j in range(2)]
        xdw2 = [B.alloc(f"xdw2_{i}", [128, 128], BF16) for i in range(2)]
        cvod2 = [[B.alloc(f"cvod_{i}_{j}", [128, 128], BF16) for i in range(NINNER_CH)]
                 for j in range(2)]
        Sall = B.alloc("Sall", [128, NINNER_CH * 128], BF16)
        yg2 = [B.alloc(f"yg2_{i}", [128, 128], BF16) for i in range(NINNER_CH)]
        ygsqb = [B.alloc(f"ygsq_{i}", [128, 128], BF16) for i in range(NINNER_CH)]
        rstdc = B.alloc("rstdc", [128, 1], F32)
        B.guard_wait()
        if stash_tok is not None:
            P.wait('g', stash_tok[0], stash_tok[1])

        tri = trile32 if d == 0 else trige32
        msk = maskf if d == 0 else maskb
        endc = 127 if d == 0 else 0

        win_toks = [wload(winb[c][:], win[d][c][:, 0:D_IN_PROJ]) for c in range(NCH)]
        cd_toks = [wload(cdiag[i][:], convdiag[d][i]) for i in range(NCONV_CH)]
        par_tok = Qy.batch([
            lambda e: e.dma_start(l1par[:, 0:18], convb[d][:, :]),
            lambda e: e.dma_start(l1par[:, 18:34], dskip[d][:, :]),
            lambda e: e.dma_start(l1par[:, 34:50], mnormw[d][:, :]),
            lambda e: e.dma_start(dtb_sb[:, :], dtbias[d][:, :]),
            lambda e: e.dma_start(alog_sb[:, 0:1], alog[d][:, :]),
        ])
        P.wait('s', par_tok[0], par_tok[1])
        P.s(lambda e: e.activation(alog_sb[:, 1:2], alog_sb[:, 0:1], AF.Exp))
        atok = si(lambda e: e.drain())
        P.wait('v', ss, atok)
        aneg_tok = vi(lambda e: e.tensor_scalar(alog_sb[:, 1:2], alog_sb[:, 1:2], -1.0, None, ALU.mult))
        vi(lambda e: e.memset(Sall[:], 0.0))
        hz0 = 0
        for i in range(NCONV_CH):
            hz0 = vi(lambda e, i=i: e.memset(rawb[i][:, 0:259], 0.0))

        IPA, IPB = pb[0], pb[1]
        GB = pb[2]
        B1B = pb[3]
        CPB = pb[4]
        YB = pb[5]
        SUB = pb[6]

        groups = list(range(8)) if d == 0 else list(range(7, -1, -1))
        lbuf_free = [0, 0]
        xdw_free = [0, 0]
        rmt_free = [0, 0]
        lexp_prev = [0, 0]
        sq_toks = [0] * NINNER_CH
        yg_pend = []
        raw_off = 3 if d == 0 else 0
        ystash_toks = []
        chunks = (0, 1) if d == 0 else (1, 0)
        feeds = {}
        scan_done = [(0, 0), (0, 0)]

        def emit_feed(g, gpar, q, war):
            t0 = g * 256
            gsl = slice(t0, t0 + 256)
            tt0 = t0 // 128
            out = {'zs': [0] * NINNER_CH, 'raw': [0] * NCONV_CH,
                   'mt': [0] * NIN_CH, 'cv': [0] * NCONV_CH, 'cvmt': 0, 'hz': 0}
            zsil = zsilp[gpar]

            def quantum_rhs():
                B.bank_write(GB)
                P.wait('t', ss, rms_tok)
                rbtok = None
                for j in range(2):
                    rep = bass.AP(rms_rstd, tt0 + j, [[16, 128], [0, 128]])
                    rbtok = ti(lambda e, r=rep, j=j: e.matmul(
                        GB.t[:, j * 128:(j + 1) * 128], r, idf[:],
                        start=True, stop=True, is_transpose=True))
                B.bank_written(GB, rbtok)
                B.bank_read(GB, 'v')
                P.wait('v', vs, max(hz0, aneg_tok))
                P.wait('v', ts, war['rhs_t'])
                rt = []
                for c in range(NCH):
                    rt.append(vi(lambda e, c=c, gsl=gsl: e.scalar_tensor_tensor(
                        rhs16[c][:], xnb[c][:, gsl], l1par[:, 50 + c:51 + c], GB.t[:, 0:256],
                        ALU.mult, ALU.mult)))
                B.bank_read_done(GB, 'v', rt[-1])
                out['rhs'] = rt
            q.append(quantum_rhs)

            for fo in range(NIN_CH):
                def quantum_ip(fo=fo):
                    bank = (IPB, GB)[fo % 2]
                    B.bank_write(bank)
                    lo = fo * 128
                    hi = min(lo + 128, D_IN_PROJ)
                    mm = hi - lo
                    mtok = None
                    for c in range(NCH):
                        P.wait('t', win_toks[c][0], win_toks[c][1])
                        P.wait('t', vs, out['rhs'][c])
                        fn = (lambda e, lo=lo, hi=hi, mm=mm, c=c, bank=bank: e.matmul(
                            bank.t[0:mm, 0:256], winb[c][:, lo:hi],
                            rhs16[c][:], start=(c == 0), stop=(c == NCH - 1)))
                        if c == NCH - 1:
                            mtok = ti(fn)
                        else:
                            P.t(fn)
                    out['mt'][fo] = mtok
                    B.bank_written(bank, mtok)
                    if fo < 16:
                        B.bank_read(bank, 's')
                        P.wait('s', vs, war['zs_v'])
                        etok = si(lambda e, fo=fo, bank=bank, zsil=zsil: e.activation(
                            zsil[fo][:], bank.t[:, 0:256], AF.Silu))
                        B.bank_read_done(bank, 's', etok)
                        out['zs'][fo] = etok
                    elif fo < 34:
                        B.bank_read(bank, 's')
                        P.wait('s', ts, war['raw_t'])
                        P.wait('s', gs, war['hz_g'])
                        etok = si(lambda e, fo=fo, bank=bank: e.activation(
                            rawb[fo - 16][:, raw_off:raw_off + 256], bank.t[:, 0:256], AF.Identity))
                        B.bank_read_done(bank, 's', etok)
                        out['raw'][fo - 16] = etok
                    else:
                        B.bank_read(bank, 'v')
                        P.wait('v', ss, war['dt_s'])
                        P.wait('v', ts, war['dtt_t'])
                        dte = vi(lambda e, bank=bank: e.tensor_copy(
                            dtbuf[:, 0:256], bank.t[0:32, 0:256]))
                        B.bank_read_done(bank, 'v', dte)
                        out['dte'] = dte
                q.append(quantum_ip)

            def quantum_dt():
                P.wait('s', vs, out['dte'])
                P.wait('s', ts, war['dtt_t'])
                P.s(lambda e: e.activation(dtbuf[:, 0:256], dtbuf[:, 0:256], AF.Exp,
                                           bias=dtb_sb[:, 0:1]))
                P.s(lambda e: e.drain())
                P.s(lambda e: e.add(dtbuf[:, 0:256], dtbuf[:, 0:256], 1.0))
                P.s(lambda e: e.drain())
                P.s(lambda e: e.activation(dtbuf[:, 0:256], dtbuf[:, 0:256], AF.Ln))
                P.s(lambda e: e.drain())
                dt_tok = si(lambda e: e.activation(dtbuf[:, 256:512], dtbuf[:, 0:256], AF.Ln))
                P.wait('v', ss, dt_tok)
                P.wait('v', vs, aneg_tok)
                P.wait('v', ts, war['dtt_t'])
                dta_tok = vi(lambda e: e.tensor_scalar(
                    dtbuf[:, 512:768], dtbuf[:, 0:256], alog_sb[:, 1:2], None, ALU.mult))
                out['dt_tok'] = dt_tok
                out['dta_tok'] = dta_tok
            q.append(quantum_dt)

            for cc in range(NCONV_CH):
                def quantum_cv(cc=cc):
                    bank = (IPB, GB)[cc % 2]
                    B.bank_write(bank)
                    P.wait('t', cd_toks[cc][0], cd_toks[cc][1])
                    P.wait('t', ss, out['raw'][cc])
                    P.wait('t', gs, war['hz_g'])
                    mtok = None
                    for j in range(D_CONV):
                        off = j if d == 0 else 3 - j
                        fn = (lambda e, cc=cc, j=j, off=off, bank=bank: e.matmul(
                            bank.t[:, 0:256], cdiag[cc][:, j * 128:(j + 1) * 128],
                            rawb[cc][:, off:off + 256], start=(j == 0), stop=(j == D_CONV - 1)))
                        if j == D_CONV - 1:
                            mtok = ti(fn)
                        else:
                            P.t(fn)
                    out['cvmt'] = mtok
                    B.bank_written(bank, mtok)
                    B.bank_read(bank, 's')
                    if cc == 17:
                        P.wait('s', ts, war['cvo17_t'])
                        P.wait('s', vs, war['cvo17_v'])
                        cdst = cvo17p[gpar]
                    else:
                        P.wait('s', ts, war['cvo_t'])
                        P.wait('s', vs, war['cvo_v'])
                        cdst = cvo[cc]
                    etok = si(lambda e, cc=cc, bank=bank, cdst=cdst: e.activation(
                        cdst[:], bank.t[:, 0:256], AF.Silu, bias=l1par[:, cc:cc + 1]))
                    B.bank_read_done(bank, 's', etok)
                    out['cv'][cc] = etok
                q.append(quantum_cv)

            def quantum_hz():
                hzt = 0
                for i in range(NCONV_CH):
                    P.wait('g', ss, out['cv'][i])
                    if d == 0:
                        hzt = gi(lambda e, i=i: e.tensor_copy(rawb[i][:, 0:3], rawb[i][:, 256:259]))
                    else:
                        hzt = gi(lambda e, i=i: e.tensor_copy(rawb[i][:, 256:259], rawb[i][:, 0:3]))
                out['hz'] = hzt
            q.append(quantum_hz)
            return out

        war0 = dict(rhs_t=0, zs_v=0, raw_t=0, hz_g=0, dt_s=0, dtt_t=0,
                    cvo_t=0, cvo_v=0, cvo17_t=0, cvo17_v=0)
        pre_q = []
        feeds[0] = emit_feed(groups[0], 0, pre_q, war0)
        for q_ in pre_q:
            q_()

        for gi_, g in enumerate(groups):
            feed = feeds[gi_]
            spar = gi_ % 2
            zsil_s = zsilp[spar]
            cvo17_s = cvo17p[spar]
            setup_out = {}
            cvd_toks = [0] * NINNER_CH

            def emit_setup(k, par, feed=feed, spar=spar, cvo17_s=cvo17_s):
                csl = slice(k * 128, (k + 1) * 128)
                sct = sct2[par]
                cpcol = cpcol2[par]
                G16 = G16_2[par]
                Bnat = Bnat2[par]
                xdnat = xdnat2[par]
                B.bank_write(B1B)
                P.wait('t', vs, feed['dta_tok'])
                P.wait('t', ss, feed['dt_tok'])
                P.t(lambda e, csl=csl: e.transpose(
                    B1B.t[:, 0:32], dtbuf[0:32, 512 + csl.start:512 + csl.stop], idf[0:32, 0:32]))
                t1 = ti(lambda e, csl=csl: e.transpose(
                    B1B.t[:, 32:64], dtbuf[0:32, 256 + csl.start:256 + csl.stop], idf[0:32, 0:32]))
                B.bank_written(B1B, t1)
                B.bank_read(B1B, 'v')
                P.wait('v', ss, lexp_prev[par])
                e1 = vi(lambda e, sct=sct: e.tensor_copy(sct[:, 0:64], B1B.t[:, 0:64]))
                B.bank_read_done(B1B, 'v', e1)
                B.bank_write(B1B)
                P.wait('t', vs, e1)
                t2 = ti(lambda e, sct=sct: e.matmul(B1B.t[:, 64:96], tri[:], sct[:, 0:32],
                                                    start=True, stop=True))
                B.bank_written(B1B, t2)
                B.bank_read(B1B, 'v')
                e2 = vi(lambda e, sct=sct: e.tensor_copy(sct[:, 64:96], B1B.t[:, 64:96]))
                B.bank_read_done(B1B, 'v', e2)
                P.v(lambda e: e.drain())
                e3 = vi(lambda e, sct=sct: e.tensor_sub(sct[:, 96:128], sct[:, 32:64], sct[:, 64:96]))
                P.wait('s', vs, e3)
                cp_tok = si(lambda e, cpcol=cpcol, sct=sct: e.activation(cpcol[:], sct[:, 64:96], AF.Exp))
                B.bank_write(GB)
                P.wait('t', ss, feed['cv'][16])
                P.wait('t', ss, feed['cv'][17])
                g1 = ti(lambda e, csl=csl, cvo17_s=cvo17_s: e.matmul(
                    GB.t[:, 0:128], cvo[16][:, csl], cvo17_s[:, csl], start=True, stop=True))
                B.bank_written(GB, g1)
                B.bank_read(GB, 'v')
                g2 = vi(lambda e, G16=G16: e.tensor_copy(G16[:], GB.t[:, 0:128]))
                B.bank_read_done(GB, 'v', g2)
                B.bank_write(tbk)
                t3 = ti(lambda e, csl=csl: e.transpose(tbk.t[:, 0:128], cvo[16][:, csl], id16[:]))
                B.bank_written(tbk, t3)
                B.bank_read(tbk, 'v')
                b1 = vi(lambda e, Bnat=Bnat: e.tensor_copy(Bnat[:], tbk.t[:, 0:128]))
                B.bank_read_done(tbk, 'v', b1)
                xd_toks = [0] * NINNER_CH
                for i in range(NINNER_CH):
                    sl0 = 128 + (i % 2) * 384
                    B.bank_write(tbk)
                    P.wait('t', ss, feed['cv'][i])
                    t4 = ti(lambda e, i=i, csl=csl, sl0=sl0: e.transpose(
                        tbk.t[:, sl0:sl0 + 128], cvo[i][:, csl], id16[:]))
                    B.bank_written(tbk, t4)
                    B.bank_read(tbk, 'v')
                    xd_toks[i] = vi(lambda e, i=i, sl0=sl0, xdnat=xdnat: e.tensor_copy(
                        xdnat[i][:], tbk.t[:, sl0:sl0 + 128]))
                    B.bank_read_done(tbk, 'v', xd_toks[i])
                    P.wait('v', ss, feed['cv'][i])
                    cvd_toks[i] = vi(lambda e, i=i, csl=csl, par=par: e.tensor_scalar(
                        cvod2[par][i][:], cvo[i][:, csl], l1par[:, 18 + i:19 + i], None, ALU.mult))
                setup_out[par] = dict(e2=e2, cp_tok=cp_tok, g2=g2, b1=b1,
                                      xd_toks=xd_toks, csl=csl, t1=t1)

            emit_setup(chunks[0], 0)
            emit_setup(chunks[1], 1)
            setup_done = (ts.n, vs.n)

            braid = []
            if gi_ + 1 < len(groups):
                war = dict(
                    rhs_t=feed['mt'][NIN_CH - 1],
                    zs_v=scan_done[(gi_ + 1) % 2][1],
                    raw_t=feed['cvmt'],
                    hz_g=feed['hz'],
                    dt_s=feed['dt_tok'],
                    dtt_t=max(setup_out[0]['t1'], setup_out[1]['t1']),
                    cvo_t=setup_done[0], cvo_v=setup_done[1],
                    cvo17_t=scan_done[(gi_ + 1) % 2][0],
                    cvo17_v=scan_done[(gi_ + 1) % 2][1])
                feeds[gi_ + 1] = emit_feed(groups[gi_ + 1], (gi_ + 1) % 2, braid, war)

            def pop_q(n):
                for _ in range(n):
                    if braid:
                        braid.pop(0)()

            for ki, k in enumerate(chunks):
                par = ki
                csl = slice(k * 128, (k + 1) * 128)
                kk = g * 2 + k
                so_ = setup_out[par]
                e2 = so_['e2']; cp_tok = so_['cp_tok']; g2 = so_['g2']; b1 = so_['b1']
                xd_toks = so_['xd_toks']
                sct = sct2[par]
                cpcol = cpcol2[par]
                G16 = G16_2[par]
                Bnat = Bnat2[par]
                xdnat = xdnat2[par]
                cvod = cvod2[par]
                if ystash_toks:
                    P.wait('v', ystash_toks[-1][0], ystash_toks[-1][1])
                def emit_b1(j4):
                    pi = j4 % 2
                    B1 = (B1B, IPA)[pi]
                    B.bank_write(B1)
                    P.wait('t', vs, e2)
                    P.wait('t', ss, cp_tok)
                    for h2 in range(4):
                        repl = bass.AP(sct, 64 + 4 * j4 + h2, [[128, 128], [0, 128]])
                        P.t(lambda e, h2=h2, r=repl, B1=B1: e.matmul(
                            B1.t[:, h2 * 128:(h2 + 1) * 128], r, idf[:],
                            start=(h2 == 0), stop=False, is_transpose=True))
                    t5 = None
                    for h2 in range(4):
                        t5 = ti(lambda e, h2=h2, B1=B1: e.matmul(
                            B1.t[:, h2 * 128:(h2 + 1) * 128], id16[:], msk[:],
                            start=False, stop=(h2 == 3)))
                    B.bank_written(B1, t5)

                def emit_cp(j4):
                    CP = CPB
                    B.bank_write(CP)
                    P.wait('t', ss, cp_tok)
                    t6 = None
                    for h2 in range(4):
                        repc = bass.AP(cpcol, 4 * j4 + h2, [[32, 128], [0, 128]])
                        t6 = ti(lambda e, h2=h2, r=repc, CP=CP: e.matmul(
                            CP.t[:, h2 * 128:(h2 + 1) * 128], r, idf[:],
                            start=True, stop=True, is_transpose=True))
                    B.bank_written(CP, t6)

                def emit_exps(j4):
                    pi = j4 % 2
                    B1 = (B1B, IPA)[pi]
                    Lp = Lp4[pi]
                    B.bank_read(B1, 's')
                    P.wait('s', vs, lbuf_free[pi])
                    l_tok = None
                    for h2 in range(4):
                        l_tok = si(lambda e, h2=h2, j4=j4, Lp=Lp, B1=B1, sct=sct: e.activation(
                            Lp[:, h2 * 128:(h2 + 1) * 128], B1.t[:, h2 * 128:(h2 + 1) * 128],
                            AF.Exp, bias=sct[:, 96 + 4 * j4 + h2:97 + 4 * j4 + h2]))
                    B.bank_read_done(B1, 's', l_tok)
                    l_toks[j4] = l_tok

                l_toks = [0] * 8
                emit_b1(0)
                emit_cp(0)
                emit_exps(0)
                for j4 in range(8):
                    pi = j4 % 2
                    B1 = (B1B, IPA)[pi]
                    CP = CPB
                    Lp = Lp4[pi]
                    GL = GL4[pi]
                    Ccp = Ccp4[pi]
                    cpe = cpe4[pi]
                    wcol = wcol4[pi]
                    l_tok = l_toks[j4]
                    if j4 + 1 < 8:
                        emit_b1(j4 + 1)
                        emit_exps(j4 + 1)
                    pop_q(2)
                    P.wait('v', ss, l_tok)
                    P.wait('v', vs, g2)
                    wrep = bass.AP(Lp, endc, [[512, 128], [128, 4]])
                    P.v(lambda e, r=wrep, wcol=wcol: e.tensor_copy(wcol[:], r))
                    P.v(lambda e: e.drain())
                    grep = bass.AP(G16, 0, [[128, 128], [0, 4], [1, 128]])
                    gl_tok = vi(lambda e, r=grep, GL=GL, Lp=Lp: e.tensor_tensor(
                        GL[:], r, Lp[:], ALU.mult))
                    B.bank_read(CP, 'v')
                    crep2 = bass.AP(cvo17_s, csl.start, [[256, 128], [0, 4], [1, 128]])
                    cc_tok = vi(lambda e, r=crep2, Ccp=Ccp, CP=CP: e.tensor_tensor(
                        Ccp[:], r, CP.t[:], ALU.mult))
                    ecp = bass.AP(CP.t, endc, [[512, 128], [128, 4]])
                    cpe_tok = vi(lambda e, r=ecp, cpe=cpe: e.tensor_copy(cpe[:], r))
                    B.bank_read_done(CP, 'v', cpe_tok)
                    pop_q(2)
                    while yg_pend:
                        yg_pend.pop(0)()
                    for p2 in range(2):
                        pair = 2 * j4 + p2
                        he, ho = 2 * p2, 2 * p2 + 1
                        YB2 = (YB, SUB)[p2]
                        B.bank_write(YB2)
                        P.wait('t', vs, max(gl_tok, cc_tok, xd_toks[pair], cvd_toks[pair]))
                        P.t(lambda e, pair=pair, he=he, GL=GL, YB2=YB2, xdnat=xdnat: e.matmul(
                            YB2.t[0:64, 0:128], xdnat[pair][:, 0:64],
                            GL[:, he * 128:(he + 1) * 128],
                            start=True, stop=False, tile_position=(0, 0)))
                        P.t(lambda e, pair=pair, he=he, Ccp=Ccp, YB2=YB2: e.matmul(
                            YB2.t[0:64, 0:128], Sall[:, pair * 128:pair * 128 + 64],
                            Ccp[:, he * 128:(he + 1) * 128],
                            start=False, stop=False, tile_position=(0, 0)))
                        P.t(lambda e, pair=pair, YB2=YB2, cvod=cvod: e.matmul(
                            YB2.t[0:64, 0:128], id16[:, 0:64], cvod[pair][:],
                            start=False, stop=True, tile_position=(0, 0)))
                        P.t(lambda e, pair=pair, ho=ho, GL=GL, YB2=YB2, xdnat=xdnat: e.matmul(
                            YB2.t[64:128, 0:128], xdnat[pair][:, 64:128],
                            GL[:, ho * 128:(ho + 1) * 128],
                            start=True, stop=False, tile_position=(0, 64)))
                        P.t(lambda e, pair=pair, ho=ho, Ccp=Ccp, YB2=YB2: e.matmul(
                            YB2.t[64:128, 0:128], Sall[:, pair * 128 + 64:pair * 128 + 128],
                            Ccp[:, ho * 128:(ho + 1) * 128],
                            start=False, stop=False, tile_position=(0, 64)))
                        y_mt = ti(lambda e, pair=pair, YB2=YB2, cvod=cvod: e.matmul(
                            YB2.t[64:128, 0:128], id16[:, 64:128], cvod[pair][:],
                            start=False, stop=True, tile_position=(0, 64)))
                        B.bank_written(YB2, y_mt)
                        P.wait('v', ts, xdw_free[p2])
                        P.v(lambda e, pair=pair, he=he, wcol=wcol, p2=p2, xdnat=xdnat: e.tensor_scalar(
                            xdw2[p2][:, 0:64], xdnat[pair][:, 0:64],
                            wcol[:, he:he + 1], None, ALU.mult))
                        xw_tok = vi(lambda e, pair=pair, ho=ho, wcol=wcol, p2=p2, xdnat=xdnat: e.tensor_scalar(
                            xdw2[p2][:, 64:128], xdnat[pair][:, 64:128],
                            wcol[:, ho:ho + 1], None, ALU.mult))
                        P.v(lambda e: e.drain())
                        SU2 = (SUB, YB)[p2]
                        B.bank_write(SU2)
                        P.wait('t', vs, max(xw_tok, b1))
                        s_mt = ti(lambda e, p2=p2, SU2=SU2, Bnat=Bnat: e.matmul(
                            SU2.t[:, 128:256], Bnat[:], xdw2[p2][:],
                            start=True, stop=True))
                        xdw_free[p2] = s_mt
                        B.bank_written(SU2, s_mt)
                        B.bank_read(SU2, 'v')
                        P.wait('v', vs, cpe_tok)
                        P.v(lambda e, pair=pair, he=he, cpe=cpe, SU2=SU2: e.scalar_tensor_tensor(
                            Sall[:, pair * 128:pair * 128 + 64], Sall[:, pair * 128:pair * 128 + 64],
                            cpe[:, he:he + 1], SU2.t[:, 128:192], ALU.mult, ALU.add))
                        su_tok = vi(lambda e, pair=pair, ho=ho, cpe=cpe, SU2=SU2: e.scalar_tensor_tensor(
                            Sall[:, pair * 128 + 64:pair * 128 + 128],
                            Sall[:, pair * 128 + 64:pair * 128 + 128],
                            cpe[:, ho:ho + 1], SU2.t[:, 192:256], ALU.mult, ALU.add))
                        B.bank_read_done(SU2, 'v', su_tok)
                        P.v(lambda e: e.drain())
                        B.bank_read(YB2, 'v')
                        P.wait('v', ss, feed['zs'][pair])
                        yg_tok = vi(lambda e, pair=pair, csl=csl, YB2=YB2, zsil_s=zsil_s: e.tensor_tensor(
                            yg2[pair][:], YB2.t[:, 0:128], zsil_s[pair][:, csl], ALU.mult))
                        B.bank_read_done(YB2, 'v', yg_tok)
                        P.v(lambda e: e.drain())
                        def sq_chain(pair=pair):
                            P.wait('v', ts, rmt_free[0])
                            sq_toks[pair] = vi(lambda e, pair=pair, sqb=ygsqb[pair]: e.tensor_tensor(
                                sqb[:], yg2[pair][:], yg2[pair][:], ALU.mult))
                            P.v(lambda e: e.drain())
                        yg_pend.append(sq_chain)
                    lbuf_free[pi] = max(gl_tok, xw_tok)
                    if j4 + 1 < 8:
                        emit_cp(j4 + 1)
                    pop_q(2)
                lexp_prev[par] = l_toks[7]
                while yg_pend:
                    yg_pend.pop(0)()
                B.bank_write(GB)
                r_mt = None
                for pair in range(NINNER_CH):
                    P.wait('t', vs, sq_toks[pair])
                    r_mt = ti(lambda e, sqb=ygsqb[pair], first=(pair == 0),
                              last=(pair == NINNER_CH - 1): e.matmul(
                        GB.t[:, 256:257], sqb[:], ones16[:, 0:1],
                        start=first, stop=last))
                rmt_free[0] = r_mt
                rmt_free[1] = r_mt
                B.bank_written(GB, r_mt)
                B.bank_read(GB, 'v')
                rv = vi(lambda e: e.tensor_scalar(rstdc[:], GB.t[:, 256:257], 1.0 / D_INNER,
                                                  None, ALU.mult))
                B.bank_read_done(GB, 'v', rv)
                P.wait('s', vs, rv)
                P.s(lambda e: e.activation(rstdc[:], rstdc[:], AF.Ln, bias=float(EPS_RMS)))
                P.s(lambda e: e.drain())
                rs_tok = si(lambda e: e.activation(rstdc[:], rstdc[:], AF.Exp, scale=-0.5))
                B.bank_write(CPB)
                P.wait('t', ss, rs_tok)
                rb2 = ti(lambda e: e.matmul(
                    CPB.t[:, 0:128], bass.AP(rstdc, 0, [[1, 128], [0, 128]]), idf[:],
                    start=True, stop=True, is_transpose=True))
                B.bank_written(CPB, rb2)
                B.bank_read(CPB, 'v')
                if ystash_toks:
                    P.wait('v', ystash_toks[-1][0], ystash_toks[-1][1])
                yn_tok = None
                for pair in range(NINNER_CH):
                    yn_tok = vi(lambda e, pair=pair: e.scalar_tensor_tensor(
                        yg2[pair][:], yg2[pair][:], l1par[:, 34 + pair:35 + pair],
                        CPB.t[:, 0:128], ALU.mult, ALU.mult))
                B.bank_read_done(CPB, 'v', yn_tok)
                P.v(lambda e: e.drain())
                P.wait('y', vs, yn_tok)
                st = Qy.batch([
                    (lambda e, pair=pair, kk=kk, d=d: e.dma_start(
                        ydram[d][pair * 128:(pair + 1) * 128, kk * 128:(kk + 1) * 128],
                        yg2[pair][:]))
                    for pair in range(NINNER_CH)])
                ystash_toks.append(st)
            for q_ in braid:
                q_()
            braid.clear()
            scan_done[spar] = (ts.n, vs.n)
        B.reset(m0)
        return ystash_toks

    def l1_mamba_out(d, ystash_toks, x_back_tok):
        m0 = B.mark()
        wob2 = [B.alloc(f"mwo_{fc}", [128, NCH * 128], BF16) for fc in range(16)]
        mid16 = [B.alloc(f"mid_{c}", [128, T], BF16) for c in range(NCH)]
        pjb = [B.alloc(f"pjt_{m}", [128, NCH * 128], BF16) for m in range(NCH)]
        B.guard_wait()
        wo_toks = [wload(wob2[fc][:], woutn[d][fc]) for fc in range(16)]
        for t_ in ystash_toks:
            P.wait('y', t_[0], t_[1])
        MB = [pb[0], pb[1]]
        ynt = [B.alloc(f"ynt_{i}", [128, 512], BF16) for i in range(16)]
        mid_tok = 0
        for tb in range(NTB):
            P.wait('y', ts, ts.n)
            yl = Qy.batch([
                (lambda e, i=i, tb=tb, d=d: e.dma_start(
                    ynt[i][:], ydram[d][i * 128:(i + 1) * 128, tb * 512:(tb + 1) * 512]))
                for i in range(16)])
            P.wait('t', yl[0], yl[1])
            for m in range(NCH):
                bank = MB[m % 2]
                B.bank_write(bank)
                P.wait('t', wo_toks[0][0], wo_toks[0][1])
                for fc in range(16):
                    P.wait('t', wo_toks[fc][0], wo_toks[fc][1])
                mtok = None
                for fc in range(16):
                    fn = (lambda e, fc=fc, m=m, bank=bank: e.matmul(
                        bank.t[:], wob2[fc][:, m * 128:(m + 1) * 128], ynt[fc][:],
                        start=(fc == 0), stop=(fc == 15)))
                    if fc == 15:
                        mtok = ti(fn)
                    else:
                        P.t(fn)
                B.bank_written(bank, mtok)
                B.bank_read(bank, 'v')
                mid_tok = vi(lambda e, m=m, tb=tb, bank=bank: e.tensor_copy(
                    mid16[m][:, tb * 512:(tb + 1) * 512], bank.t[:]))
                B.bank_read_done(bank, 'v', mid_tok)
        pj_toks = [wload(pjb[m][:], projt[d][m]) for m in range(NCH)]
        P.wait('v', x_back_tok[0], x_back_tok[1])
        out_tok = 0
        for c in range(NCH):
            for tb in range(NTB):
                bank = MB[tb % 2]
                B.bank_write(bank)
                for m in range(NCH):
                    P.wait('t', pj_toks[m][0], pj_toks[m][1])
                P.wait('t', vs, mid_tok)
                mtok = None
                for m in range(NCH):
                    fn = (lambda e, m=m, c=c, tb=tb, bank=bank: e.matmul(
                        bank.t[:], pjb[m][:, c * 128:(c + 1) * 128],
                        mid16[m][:, tb * 512:(tb + 1) * 512],
                        start=(m == 0), stop=(m == NCH - 1)))
                    if m == NCH - 1:
                        mtok = ti(fn)
                    else:
                        P.t(fn)
                B.bank_written(bank, mtok)
                B.bank_read(bank, 'v')
                out_tok = vi(lambda e, c=c, tb=tb, bank=bank: e.tensor_add(
                    x0[c][:, tb * 512:(tb + 1) * 512], x0[c][:, tb * 512:(tb + 1) * 512],
                    bank.t[:]))
                B.bank_read_done(bank, 'v', out_tok)
                P.v(lambda e: e.drain())
        B.reset(m0)
        return out_tok

    def moe_pre(xn_tok):
        m0 = B.mark()
        xts = [B.alloc(f"xts_{i}", [128, D], BF16) for i in range(2)]
        zrow = B.alloc("zrow", [128, D], BF16)
        B.guard_wait()
        z_tok = vi(lambda e: e.memset(zrow[:], 0.0))
        P.wait('y', vs, z_tok)
        ztoks = None
        for k in range(T // 128):
            ztoks = Qy.batch([(lambda e, k=k: e.dma_start(
                bass.AP(ymoe_h, k * 128 * D, [[D, 128], [1, D]]), zrow[:]))])
        ztoks = Qy.batch([
            lambda e: e.dma_start(bass.AP(ymoe_h, T * D, [[D, 1], [1, D]]), zrow[0:1, :]),
            lambda e: e.dma_start(bass.AP(xnt_h, T * D, [[D, 1], [1, D]]), zrow[0:1, :]),
        ])
        P.wait('t', vs, xn_tok)
        dtok = None
        dprev = [None, None]
        for tc in range(16):
            buf = xts[tc % 2]
            B.bank_write(tbk)
            mt = None
            for fc in range(NCH):
                fn = (lambda e, fc=fc, tc=tc: e.matmul(
                    tbk.t[:, fc * 128:(fc + 1) * 128],
                    xnb[fc][:, tc * 128:(tc + 1) * 128], id16[:], start=True, stop=True,
                    is_transpose=True))
                mt = ti(fn) if fc == NCH - 1 else P.t(fn)
            B.bank_written(tbk, mt)
            B.bank_read(tbk, 'v')
            if dprev[tc % 2] is not None:
                P.wait('v', dprev[tc % 2][0], dprev[tc % 2][1])
            ev = vi(lambda e, buf=buf: e.tensor_copy(buf[:, 0:1024], tbk.t[:, 0:1024]))
            B.bank_read_done(tbk, 'v', ev)
            P.v(lambda e: e.drain())
            P.wait('y', vs, ev)
            dtok = Qy.batch([(lambda e, tc=tc, buf=buf: e.dma_start(
                bass.AP(xnt_h, tc * D, [[16 * D, 128], [1, D]]), buf[:]))])
            dprev[tc % 2] = dtok
        B.reset(m0)
        return dtok, ztoks

    def l1_moe(xn_tok, x_back_tok, pre_toks):
        m0 = B.mark()
        MFD = 264
        CAP = 1024
        rwb = B.alloc("rwb", [128, 8 * NCH], BF16)
        rtmp = B.alloc("rtmp", [128, 72], F32)
        topk_sb = B.alloc("topk_sb", [128, 16, 8], F32)
        argtf = B.alloc("argtf", [128, 16 * 8], F32)
        argt_sb = B.alloc("argt_sb", [128, 16, 8], U32)
        sid_sb = B.alloc("sid_sb", [1, 8], U16)
        gat_sb = [B.alloc(f"gat_{i}", [128, MFD], F32) for i in range(8)]
        bidx_sb = [B.alloc(f"bidx_{i}", [128, MFD], I16) for i in range(8)]
        cidx_sb = B.alloc("cidx_sb", [128, MFD], I16)
        cnt_sb = [B.alloc(f"cnt_{i}", [128, 1], U32) for i in range(8)]
        gmod = [B.alloc(f"gmod_{i}", [128, 8], F32) for i in range(8)]
        w13b = B.alloc("w13b", [128, 16 * 1024], BF16)
        w2b = B.alloc("w2b", [128, 8 * 1024], BF16)
        xg = B.alloc("xg", [128, 8, 1024], BF16)
        ysw_off = (B.sbuf_cursor + 63) // 64 * 64
        ysw = B.alloc("ysw", [128, 8, 1024], BF16)
        ge16 = [B.alloc(f"ge16_{hc}", [128, 512], BF16) for hc in range(8)]
        s16 = B.alloc("moe_s16", [128, 512], BF16)
        B.overlay_region(ysw_off, ysw_off + 16384)
        ybk2 = [B.alloc(f"ybk_{i}", [128, 1024], BF16, overlay=True) for i in range(2)]
        B.guard_wait()
        RA, RB, RC, RD = pb[0], pb[1], pb[2], pb[3]
        rw_tok = Qg.batch([(lambda e, c=c: e.dma_start(
            rwb[:, c * 8:(c + 1) * 8], routerw[c])) for c in range(NCH)])
        eb_tok = Qy.batch([lambda e: e.dma_start(rtmp[:, 40:48], ebias_rep[:, :]),
                           lambda e: e.dma_start(rtmp[:, 64:72], iota8[:, :]),
                           lambda e: e.dma_start(sid_sb[:, :], sidx[:, :])])
        z1 = vi(lambda e: e.memset(bass.AP(topk_sb, 0, [[128, 128], [1, 128]]), 0.0))
        z2 = vi(lambda e: e.memset(argtf[:], 0.0))
        P.v(lambda e: e.drain())
        P.wait('t', rw_tok[0], rw_tok[1])
        P.wait('t', vs, xn_tok)
        P.wait('v', eb_tok[0], eb_tok[1])
        for tt in range(NTT):
            bank = (RA, RB)[tt % 2]
            B.bank_write(bank)
            mtok = None
            for c in range(NCH):
                fn = (lambda e, c=c, tt=tt, bank=bank: e.matmul(
                    bank.t[:, 0:8], xnb[c][:, tt * 128:(tt + 1) * 128],
                    rwb[:, c * 8:(c + 1) * 8], start=(c == 0), stop=(c == NCH - 1)))
                if c == NCH - 1:
                    mtok = ti(fn)
                else:
                    P.t(fn)
            B.bank_written(bank, mtok)
            B.bank_read(bank, 'v')
            P.v(lambda e, bank=bank: e.reduce_max(rtmp[:, 32:33], bank.t[:, 0:8],
                                                  axis=AX.X, negate=True))
            nm_tok = vi(lambda e: e.drain())
            B.bank_read(bank, 's')
            P.wait('s', vs, nm_tok)
            ex_tok = si(lambda e, bank=bank: e.activation(
                rtmp[:, 0:8], bank.t[:, 0:8], AF.Exp, bias=rtmp[:, 32:33]))
            B.bank_read_done(bank, 's', ex_tok)
            B.bank_read_done(bank, 'v', nm_tok)
            P.wait('v', ss, ex_tok)
            P.v(lambda e: e.reduce_sum(rtmp[:, 33:34], rtmp[:, 0:8], axis=AX.X))
            P.v(lambda e: e.drain())
            P.v(lambda e: e.reciprocal(rtmp[:, 34:35], rtmp[:, 33:34]))
            P.v(lambda e: e.drain())
            P.v(lambda e: e.tensor_scalar(rtmp[:, 8:16], rtmp[:, 0:8], rtmp[:, 34:35],
                                          None, ALU.mult))
            P.v(lambda e: e.drain())
            P.v(lambda e: e.tensor_add(rtmp[:, 16:24], rtmp[:, 8:16], rtmp[:, 40:48]))
            P.v(lambda e: e.drain())
            P.v(lambda e: e.reduce_max(rtmp[:, 35:36], rtmp[:, 16:24], axis=AX.X))
            P.v(lambda e: e.drain())
            P.v(lambda e: e.tensor_scalar(rtmp[:, 24:32], rtmp[:, 16:24], rtmp[:, 35:36],
                                          None, ALU.is_ge))
            P.v(lambda e: e.drain())
            P.v(lambda e: e.tensor_copy(rtmp[:, 48:56], rtmp[:, 24:32]))
            P.v(lambda e: e.drain())
            P.v(lambda e: e.scalar_tensor_tensor(rtmp[:, 24:32], rtmp[:, 24:32], -1e9,
                                                 rtmp[:, 16:24], ALU.mult, ALU.add))
            P.v(lambda e: e.drain())
            P.v(lambda e: e.reduce_max(rtmp[:, 36:37], rtmp[:, 24:32], axis=AX.X))
            P.v(lambda e: e.drain())
            P.v(lambda e: e.tensor_scalar(rtmp[:, 24:32], rtmp[:, 16:24], rtmp[:, 36:37],
                                          None, ALU.is_ge))
            P.v(lambda e: e.drain())
            P.wait('v', vs, max(z1, z2))
            for k2, msl in ((0, slice(48, 56)), (1, slice(24, 32))):
                P.v(lambda e, msl=msl: e.tensor_mul(rtmp[:, 56:64], rtmp[:, msl],
                                                    rtmp[:, 8:16]))
                P.v(lambda e: e.drain())
                P.v(lambda e, tt=tt, k2=k2: e.reduce_sum(
                    bass.AP(topk_sb, tt * 8 + k2, [[128, 128], [1, 1]]),
                    rtmp[:, 56:64], axis=AX.X))
                P.v(lambda e: e.drain())
                P.v(lambda e, msl=msl: e.tensor_mul(rtmp[:, 56:64], rtmp[:, msl],
                                                    rtmp[:, 64:72]))
                P.v(lambda e: e.drain())
                P.v(lambda e, tt=tt, k2=k2: e.reduce_sum(
                    argtf[:, tt * 8 + k2:tt * 8 + k2 + 1], rtmp[:, 56:64], axis=AX.X))
                P.v(lambda e: e.drain())
        last_v = vi(lambda e: e.drain())
        P.wait('g', vs, last_v)
        cast_tok = gi(lambda e: e.tensor_copy(
            bass.AP(argt_sb, 0, [[128, 128], [1, 128]]), argtf[:, :]))
        P.g(lambda e: e.drain())
        P.wait('g', eb_tok[0], eb_tok[1])

        ig_last = 0
        for e8 in range(8):
            ig_last = gi(lambda e, e8=e8: e.tensor_copy(cnt_sb[e8][:, :], cnt_sb[e8][:, :]))
            P.g(lambda e: e.drain())
        cl_tok = gs.n
        gm_toks = []
        P.wait('y', gs, ig_last)
        for e8 in range(8):
            gm_toks.append(Qy.batch([
                (lambda e, e8=e8, q=q: e.dma_start(
                    gmod[e8][16 * q:16 * (q + 1), 0:8],
                    bass.AP(gat_sb[e8], q, [[MFD, 16], [8, 8], [1, 1]]),
                    allow_slow_non_contiguous=True))
                for q in range(8)]))
        dump_tok, zero_tok = pre_toks
        P.wait('g', dump_tok[0], dump_tok[1])
        xz_tok = vi(lambda e: e.memset(bass.AP(xg, 0, [[8192, 128], [1, 8192]]), 0.0))
        P.v(lambda e: e.drain())
        P.wait('g', vs, xz_tok)
        g_slot = Qg.batch([lambda e: e.dma_start(
            bass.AP(xg, 0, [[8192, 128], [1, 8192]]),
            bass.AP(xnt_h, 0, [[D, 128], [1, 8192]]))])
        w13_tok = Qg.batch([(lambda e, wi=wi, hc=hc: e.dma_start(
            w13b[:, (wi * 8 + hc) * 1024:(wi * 8 + hc + 1) * 1024], w13[0][wi][hc]))
            for wi in range(2) for hc in range(8)])
        w2_tok = Qg.batch([(lambda e, hc=hc: e.dma_start(
            w2b[:, hc * 1024:(hc + 1) * 1024], w2n[0][hc])) for hc in range(NCH)])
        sc_slot = None
        w2_read_done = 0
        ysw_free = 0
        for e8 in range(8):
            P.wait('t', g_slot[0], g_slot[1])
            P.wait('t', w13_tok[0], w13_tok[1])
            h_last = 0
            for tb2 in range(2):
                sl = slice(tb2 * 512, (tb2 + 1) * 512)
                prev_g = 0
                for hc in range(8):
                    B.bank_write(RA)
                    mta = None
                    for dc in range(NCH):
                        fn = (lambda e, hc=hc, dc=dc, sl=sl: e.matmul(
                            RA.t[:], w13b[:, (0 * 8 + hc) * 1024 + dc * 128:
                                          (0 * 8 + hc) * 1024 + (dc + 1) * 128],
                            bass.AP(xg, dc * 1024 + sl.start, [[8192, 128], [1, 512]]),
                            start=(dc == 0), stop=(dc == NCH - 1)))
                        if dc == NCH - 1:
                            mta = ti(fn)
                        else:
                            P.t(fn)
                    B.bank_written(RA, mta)
                    B.bank_write(RB)
                    mtb = None
                    for dc in range(NCH):
                        fn = (lambda e, hc=hc, dc=dc, sl=sl: e.matmul(
                            RB.t[:], w13b[:, (1 * 8 + hc) * 1024 + dc * 128:
                                          (1 * 8 + hc) * 1024 + (dc + 1) * 128],
                            bass.AP(xg, dc * 1024 + sl.start, [[8192, 128], [1, 512]]),
                            start=(dc == 0), stop=(dc == NCH - 1)))
                        if dc == NCH - 1:
                            mtb = ti(fn)
                        else:
                            P.t(fn)
                    B.bank_written(RB, mtb)
                    B.bank_read(RA, 's')
                    P.wait('s', vs, prev_g)
                    sl_tok = si(lambda e: e.activation(s16[:], RA.t[:], AF.Silu))
                    B.bank_read_done(RA, 's', sl_tok)
                    B.bank_read(RB, 'v')
                    P.wait('v', ss, sl_tok)
                    P.wait('v', ts, w2_read_done)
                    g_tok = vi(lambda e, hc=hc: e.tensor_tensor(
                        ge16[hc][:], s16[:], RB.t[:], ALU.mult))
                    B.bank_read_done(RB, 'v', g_tok)
                    P.v(lambda e: e.drain())
                    prev_g = g_tok
                    h_last = max(h_last, mta, mtb)
                h_tok = vs.n
                P.wait('t', w2_tok[0], w2_tok[1])
                P.wait('t', vs, h_tok)
                for j2 in range(4):
                    j = tb2 * 4 + j2
                    for half in range(2):
                        bank = (RC, RD)[half]
                        B.bank_write(bank)
                        mtok = None
                        for hc in range(8):
                            fn = (lambda e, hc=hc, j2=j2, half=half, bank=bank: e.matmul(
                                bank.t[:], ge16[hc][:, j2 * 128:(j2 + 1) * 128],
                                w2b[:, hc * 1024 + half * 512:hc * 1024 + half * 512 + 512],
                                start=(hc == 0), stop=(hc == 7)))
                            if hc == 7:
                                mtok = ti(fn)
                            else:
                                P.t(fn)
                        B.bank_written(bank, mtok)
                        B.bank_read(bank, 'v')
                        P.wait('v', gm_toks[e8][0], gm_toks[e8][1])
                        if sc_slot is not None:
                            P.wait('v', sc_slot[0], sc_slot[1])
                        ot = vi(lambda e, e8=e8, j=j, half=half, bank=bank: e.tensor_scalar(
                            bass.AP(ysw, j * 1024 + half * 512, [[8192, 128], [1, 512]]),
                            bank.t[:], gmod[e8][:, j:j + 1], None, ALU.mult))
                        B.bank_read_done(bank, 'v', ot)
                        P.v(lambda e: e.drain())
                        ysw_free = ot
            w2_read_done = ts.n
            if e8 < 7:
                P.wait('g', ts, h_last)
                P.wait('v', ts, h_last)
                xz_tok = vi(lambda e: e.memset(bass.AP(xg, 0, [[8192, 128], [1, 8192]]), 0.0))
                P.v(lambda e: e.drain())
                P.wait('g', vs, xz_tok)
                g_slot = Qg.batch([(lambda e, e8=e8: e.dma_start(
                    bass.AP(xg, 0, [[8192, 128], [1, 8192]]),
                    bass.AP(xnt_h, 0, [[D, 128], [1, 8192]])))])
                w13_tok = Qg.batch([(lambda e, wi=wi, hc=hc, en=e8 + 1: e.dma_start(
                    w13b[:, (wi * 8 + hc) * 1024:(wi * 8 + hc + 1) * 1024],
                    w13[en][wi][hc])) for wi in range(2) for hc in range(8)])
                P.wait('g', ts, w2_read_done)
                w2_tok = Qg.batch([(lambda e, hc=hc, en=e8 + 1: e.dma_start(
                    w2b[:, hc * 1024:(hc + 1) * 1024], w2n[en][hc])) for hc in range(NCH)])
            else:
                P.wait('g', ts, h_last)
                w13_tok = Qg.batch([(lambda e, wi=wi, hc=hc: e.dma_start(
                    w13b[:, (wi * 8 + hc) * 1024:(wi * 8 + hc + 1) * 1024],
                    w13[8][wi][hc])) for wi in range(2) for hc in range(8)])
                P.wait('g', ts, w2_read_done)
                w2_tok = Qg.batch([(lambda e, dc=dc: e.dma_start(
                    w2b[:, dc * 1024:(dc + 1) * 1024], w2[8][dc])) for dc in range(NCH)])
            P.wait('g', vs, ysw_free)
            if sc_slot is not None:
                P.wait('g', sc_slot[0], sc_slot[1])
            P.wait('g', zero_tok[0], zero_tok[1])
            sc_slot = Qg.batch([(lambda e, e8=e8: e.dma_start(
                bass.AP(ymoe_h, 0, [[D, 128], [1, 8192]]),
                bass.AP(ysw, 0, [[8192, 128], [1, 8192]])))])
        P.wait('t', w13_tok[0], w13_tok[1])
        for tb in range(NTB):
            prev_g = 0
            for hc in range(8):
                B.bank_write(RA)
                mta = None
                for dc in range(NCH):
                    fn = (lambda e, hc=hc, dc=dc, tb=tb: e.matmul(
                        RA.t[:], w13b[:, (0 * 8 + hc) * 1024 + dc * 128:
                                      (0 * 8 + hc) * 1024 + (dc + 1) * 128],
                        xnb[dc][:, tb * 512:(tb + 1) * 512],
                        start=(dc == 0), stop=(dc == NCH - 1)))
                    if dc == NCH - 1:
                        mta = ti(fn)
                    else:
                        P.t(fn)
                B.bank_written(RA, mta)
                B.bank_write(RB)
                mtb = None
                for dc in range(NCH):
                    fn = (lambda e, hc=hc, dc=dc, tb=tb: e.matmul(
                        RB.t[:], w13b[:, (1 * 8 + hc) * 1024 + dc * 128:
                                      (1 * 8 + hc) * 1024 + (dc + 1) * 128],
                        xnb[dc][:, tb * 512:(tb + 1) * 512],
                        start=(dc == 0), stop=(dc == NCH - 1)))
                    if dc == NCH - 1:
                        mtb = ti(fn)
                    else:
                        P.t(fn)
                B.bank_written(RB, mtb)
                B.bank_read(RA, 's')
                P.wait('s', vs, prev_g)
                sl_tok = si(lambda e: e.activation(s16[:], RA.t[:], AF.Silu))
                B.bank_read_done(RA, 's', sl_tok)
                B.bank_read(RB, 'v')
                P.wait('v', ss, sl_tok)
                P.wait('v', ts, w2_read_done)
                g_tok = vi(lambda e, hc=hc: e.tensor_tensor(
                    ge16[hc][:], s16[:], RB.t[:], ALU.mult))
                B.bank_read_done(RB, 'v', g_tok)
                P.v(lambda e: e.drain())
                prev_g = g_tok
            h_tok = vs.n
            P.wait('t', w2_tok[0], w2_tok[1])
            P.wait('t', vs, h_tok)
            if tb == 0:
                P.wait('v', x_back_tok[0], x_back_tok[1])
            for dc in range(NCH):
                bank = (RC, RD)[dc % 2]
                B.bank_write(bank)
                mtok = None
                for hc in range(8):
                    fn = (lambda e, dc=dc, hc=hc, bank=bank: e.matmul(
                        bank.t[:], w2b[:, dc * 1024 + hc * 128:dc * 1024 + (hc + 1) * 128],
                        ge16[hc][:],
                        start=(hc == 0), stop=(hc == 7)))
                    if hc == 7:
                        mtok = ti(fn)
                    else:
                        P.t(fn)
                B.bank_written(bank, mtok)
                B.bank_read(bank, 'v')
                ot = vi(lambda e, dc=dc, tb=tb, bank=bank: e.tensor_add(
                    x0[dc][:, tb * 512:(tb + 1) * 512],
                    x0[dc][:, tb * 512:(tb + 1) * 512], bank.t[:]))
                B.bank_read_done(bank, 'v', ot)
                P.v(lambda e: e.drain())
            w2_read_done = ts.n
        P.wait('y', sc_slot[0], sc_slot[1])
        moe_tok = 0
        mt_prev = [0, 0]
        for tc in range(16):
            ybk = ybk2[tc % 2]
            P.wait('y', ts, mt_prev[tc % 2])
            yb_tok = Qy.batch([(lambda e, tc=tc, ybk=ybk: e.dma_start(
                ybk[:], bass.AP(ymoe_h, tc * D, [[16 * D, 128], [1, D]])))])
            B.bank_write(tbk)
            P.wait('t', yb_tok[0], yb_tok[1])
            mt = None
            for fc in range(NCH):
                fn = (lambda e, fc=fc, ybk=ybk: e.matmul(
                    tbk.t[:, fc * 128:(fc + 1) * 128], ybk[:, fc * 128:(fc + 1) * 128],
                    id16[:], start=True, stop=True, is_transpose=True))
                mt = ti(fn) if fc == NCH - 1 else P.t(fn)
            B.bank_written(tbk, mt)
            mt_prev[tc % 2] = mt
            B.bank_read(tbk, 'v')
            at = 0
            for fc in range(NCH):
                at = vi(lambda e, fc=fc, tc=tc: e.tensor_add(
                    x0[fc][:, tc * 128:(tc + 1) * 128],
                    x0[fc][:, tc * 128:(tc + 1) * 128], tbk.t[:, fc * 128:(fc + 1) * 128]))
            B.bank_read_done(tbk, 'v', at)
            P.v(lambda e: e.drain())
            moe_tok = at
        B.reset(m0)
        return moe_tok

    init_consts()
    x_tok = load_x()
    ln_tok, _ = layer_norm(x_tok, stash=False)

    tp = tap("xnb", [D, T], BF16)
    if tp is not None:
        P.wait('y', vs, ln_tok)
        Qy.batch([
            (lambda e, c=c, tp=tp: e.dma_start(tp[c * 128:(c + 1) * 128, :], xnb[c][:]))
            for c in range(NCH)])

    attn_tok = l0_attention(ln_tok)
    ffn_tok = l0_ffn(ln_tok)

    ln2_tok, _ = layer_norm((vs, vs.n), stash=False)
    rms_tok, resid_tok = l1_prep(ln2_tok)
    pre_toks = moe_pre(ln2_tok)
    x0_dram = nc.dram_tensor("x0_stash", [D, T], F32, kind="Internal").ap()
    P.wait('y', vs, vs.n)
    x0_st = Qy.batch([
        (lambda e, c=c: e.dma_start(x0_dram[c * 128:(c + 1) * 128, :], x0[c][:]))
        for c in range(NCH)])
    yst0 = l1_mamba(0, rms_tok, stash_tok=x0_st)
    yst1 = l1_mamba(1, rms_tok, stash_tok=x0_st)
    P.wait('y', x0_st[0], x0_st[1])
    B.guard_wait(engines="y")
    x_back = Qy.batch([
        (lambda e, c=c: e.dma_start(x0[c][:], x0_dram[c * 128:(c + 1) * 128, :]))
        for c in range(NCH)])
    mo0 = l1_mamba_out(0, yst0, x_back)
    mo1 = l1_mamba_out(1, yst1, x_back)
    moe_tok = l1_moe(ln2_tok, x_back, pre_toks)

    tpm = tap("x_after_mamba", [D, T])
    if tpm is not None:
        P.wait('y', vs, vs.n)
        Qy.batch([
            (lambda e, c=c, tp=tpm: e.dma_start(tp[c * 128:(c + 1) * 128, :], x0[c][:]))
            for c in range(NCH)])

    tpl0 = tap("x_after_l0", [D, T])
    if tpl0 is not None:
        P.wait('y', vs, vs.n)
        Qy.batch([
            (lambda e, c=c, tp=tpl0: e.dma_start(tp[c * 128:(c + 1) * 128, :], x0[c][:]))
            for c in range(NCH)])

    tpx = tap("x_after_attn", [D, T])
    if tpx is not None:
        P.wait('y', vs, vs.n)
        Qy.batch([
            (lambda e, c=c, tp=tpx: e.dma_start(tp[c * 128:(c + 1) * 128, :], x0[c][:]))
            for c in range(NCH)])

    P.wait('y', vs, vs.n)
    Qy.batch([
        (lambda e, c=c: e.dma_start(out[c * 128:(c + 1) * 128, :], x0[c][:]))
        for c in range(NCH)])
    for s_ in Qy.slots:
        if s_.n:
            P.wait('y', s_, s_.n)
    for s_ in Qg.slots:
        if s_.n:
            P.wait('y', s_, s_.n)

    with nc.Block() as block:
        P.run(block)
    return nc, taps


def _prep_shared(inputs):
    f = np.float32
    g = {}
    Wq, Wk, Wv, Wo = (np.asarray(inputs[k], f) for k in
                      ("L0_attn_Wq", "L0_attn_Wk", "L0_attn_Wv", "L0_attn_Wo"))
    wqkv = np.zeros((8, 128, 3 * NCH * 128), f)
    for p in range(8):
        for wi, W in enumerate((Wq, Wk, Wv)):
            for c in range(NCH):
                wqkv[p, :, (wi * NCH + c) * 128:(wi * NCH + c + 1) * 128] = \
                    W[c * 128:(c + 1) * 128, p * 128:(p + 1) * 128]
    g["wqkv"] = wqkv
    bq, bk, bv, bo = (np.asarray(inputs[k], f) for k in
                      ("L0_attn_bq", "L0_attn_bk", "L0_attn_bv", "L0_attn_bo"))
    bqkv = np.zeros((128, 4 * NCH), f)
    for c in range(NCH):
        for j, b in enumerate((bq, bk, bv, bo)):
            bqkv[:, 4 * c + j] = b[c * 128:(c + 1) * 128]
    g["bqkv"] = bqkv
    wo_p = np.zeros((NCH, 128, 8 * 128), f)
    for c in range(NCH):
        for p in range(8):
            wo_p[c, :, p * 128:(p + 1) * 128] = Wo[p * 128:(p + 1) * 128, c * 128:(c + 1) * 128]
    g["wo"] = wo_p
    WfcT = np.ascontiguousarray(np.asarray(inputs["L0_ffn_W_fc"], f).T)
    WprojT = np.ascontiguousarray(np.asarray(inputs["L0_ffn_W_proj"], f).T)
    wfc = np.zeros((32, 128, NCH * 128), f)
    for fc in range(32):
        for c in range(NCH):
            wfc[fc, :, c * 128:(c + 1) * 128] = WfcT[c * 128:(c + 1) * 128, fc * 128:(fc + 1) * 128]
    g["wfc"] = wfc
    wproj = np.zeros((NCH, 128, 32 * 128), f)
    for c in range(NCH):
        for fc in range(32):
            wproj[c, :, fc * 128:(fc + 1) * 128] = WprojT[fc * 128:(fc + 1) * 128, c * 128:(c + 1) * 128]
    g["wproj"] = wproj
    for d, pre in enumerate(("fwd", "bwd")):
        WinT = np.ascontiguousarray(np.asarray(inputs[f"L1_attn_{pre}_W_in"], f).T)
        winp = np.zeros((NCH, 128, NIN_CH * 128), f)
        for c in range(NCH):
            winp[c, :, 0:D_IN_PROJ] = WinT[c * 128:(c + 1) * 128, :]
        g[f"win{d}"] = winp
        cw = np.asarray(inputs[f"L1_attn_{pre}_conv_w"], f)
        cd = np.zeros((NCONV_CH, 128, D_CONV * 128), f)
        for cc in range(NCONV_CH):
            for j in range(D_CONV):
                np.fill_diagonal(cd[cc, :, j * 128:(j + 1) * 128], cw[cc * 128:(cc + 1) * 128, j])
        g[f"convdiag{d}"] = cd
        g[f"convb{d}"] = np.ascontiguousarray(
            np.asarray(inputs[f"L1_attn_{pre}_conv_b"], f).reshape(NCONV_CH, 128).T)
        g[f"dtbias{d}"] = np.asarray(inputs[f"L1_attn_{pre}_dt_bias"], f).reshape(NHEADS, 1)
        g[f"alog{d}"] = np.asarray(inputs[f"L1_attn_{pre}_A_log"], f).reshape(NHEADS, 1)
        g[f"dskip{d}"] = np.ascontiguousarray(
            np.repeat(np.asarray(inputs[f"L1_attn_{pre}_D"], f), HEADDIM).reshape(NINNER_CH, 128).T)
        g[f"mnormw{d}"] = np.ascontiguousarray(
            np.asarray(inputs[f"L1_attn_{pre}_norm_w"], f).reshape(NINNER_CH, 128).T)
        WoutT = np.ascontiguousarray(np.asarray(inputs[f"L1_attn_{pre}_W_out"], f).T)
        wout = np.zeros((16, 128, NCH * 128), f)
        for fc in range(16):
            wout[fc] = WoutT[fc * 128:(fc + 1) * 128, :]
        g[f"woutn{d}"] = wout
        projW = np.asarray(inputs["L1_attn_proj_W"], f)
        partT = np.ascontiguousarray(projW[:, d * D:(d + 1) * D].T)
        pj = np.zeros((NCH, 128, NCH * 128), f)
        for m in range(NCH):
            pj[m] = partT[m * 128:(m + 1) * 128, :]
        g[f"projt{d}"] = pj
    g["l1normw"] = np.ascontiguousarray(
        np.asarray(inputs["L1_attn_norm_weight"], f).reshape(NCH, 128).T)
    g["projb"] = np.ascontiguousarray(
        np.asarray(inputs["L1_attn_proj_b"], f).reshape(NCH, 128).T)
    g["routerw"] = np.asarray(inputs["L1_ffn_router_w"], f).reshape(NCH, 128, 8)
    g["ebias_rep"] = np.tile(np.asarray(inputs["L1_ffn_expert_bias"], f)[None, :], (128, 1))
    w1 = np.asarray(inputs["L1_ffn_w1"], f)
    w2i = np.asarray(inputs["L1_ffn_w2"], f)
    w3 = np.asarray(inputs["L1_ffn_w3"], f)
    sw1 = np.asarray(inputs["L1_ffn_sw1"], f)
    sw2 = np.asarray(inputs["L1_ffn_sw2"], f)
    sw3 = np.asarray(inputs["L1_ffn_sw3"], f)
    w13p = np.zeros((9, 2, 8, 128, 8 * 128), f)
    w2p = np.zeros((9, 8, 128, 8 * 128), f)
    for e in range(9):
        m1 = w1[e] if e < 8 else sw1
        m3 = w3[e] if e < 8 else sw3
        m2 = w2i[e] if e < 8 else sw2
        for hc in range(8):
            for dc in range(8):
                w13p[e, 0, hc, :, dc * 128:(dc + 1) * 128] = m1[dc * 128:(dc + 1) * 128, hc * 128:(hc + 1) * 128]
                w13p[e, 1, hc, :, dc * 128:(dc + 1) * 128] = m3[dc * 128:(dc + 1) * 128, hc * 128:(hc + 1) * 128]
                w2p[e, dc, :, hc * 128:(hc + 1) * 128] = m2[hc * 128:(hc + 1) * 128, dc * 128:(dc + 1) * 128]
    g["w13"] = w13p
    g["w2"] = w2p
    w2np = np.zeros((8, 8, 128, 8 * 128), f)
    for e in range(8):
        for hc in range(8):
            w2np[e, hc] = w2i[e][hc * 128:(hc + 1) * 128, :]
    g["w2n"] = w2np
    g["iota8"] = np.tile(np.arange(8, dtype=f)[None, :], (128, 1))
    g["sidx"] = np.arange(8, dtype=np.uint16).reshape(1, 8)
    import ml_dtypes
    idx = np.arange(128)
    g["tri_le16"] = (idx[:, None] <= idx[None, :]).astype(ml_dtypes.bfloat16)
    g["tri_ge16"] = (idx[:, None] >= idx[None, :]).astype(ml_dtypes.bfloat16)
    g["ident16"] = np.eye(128, dtype=ml_dtypes.bfloat16)
    g["identf"] = np.eye(128, dtype=np.float32)
    mfwd = np.where(idx[:, None] > idx[None, :], -1e9, 0.0)
    mbwd = np.where(idx[:, None] < idx[None, :], -1e9, 0.0)
    g["maskf16"] = mfwd.astype(ml_dtypes.bfloat16)
    g["maskb16"] = mbwd.astype(ml_dtypes.bfloat16)
    g["tri_le32"] = (idx[:, None] <= idx[None, :]).astype(f)
    g["tri_ge32"] = (idx[:, None] >= idx[None, :]).astype(f)
    return g


def kernel(**inputs):
    x = np.asarray(inputs["L0_attn_x"], np.float32)
    shared = _prep_shared(inputs)
    nc, _ = build_program()
    in_maps = []
    for core in range(8):
        m = dict(shared)
        m["xT"] = np.ascontiguousarray(x[core].T)
        in_maps.append(m)
    res = run_bass_kernel_spmd(nc, in_maps, core_ids=list(range(8)))
    outs = [np.asarray(r["outT"]).T for r in res.results]
    return np.ascontiguousarray(np.stack(outs, 0))
